# version 20
# baseline (speedup 1.0000x reference)
import contextlib
import numpy as np
import ml_dtypes
import concourse.bass as bass
import concourse.mybir as mybir
from concourse.bass_utils import run_bass_kernel_spmd

F32 = mybir.dt.float32
BF16 = mybir.dt.bfloat16
I32 = mybir.dt.int32
I8 = mybir.dt.int8
AF = mybir.ActivationFunctionType
ALU = mybir.AluOpType
AX = mybir.AxisListType

REAL = dict(NB=2, S=2048, D=1024, MFEAT=256, IFFN=2816, H2=4096)


class Buf:
    __slots__ = ("w", "r")
    def __init__(self):
        self.w = None
        self.r = []

class DmaSem:
    __slots__ = ("key", "handle", "uses")
    def __init__(self, key, handle):
        self.key, self.handle, self.uses = key, handle, 0

ENGS = ("t", "v", "s", "y", "g")

class Prog:
    def __init__(self, nc, stack, n_dma=(10, 6, 6)):
        self.nc = nc
        self.body = {e: [] for e in ENGS}
        self.sem = {}
        for e in ("t", "v", "s", "g"):
            self.sem[e] = stack.enter_context(nc.semaphore(f"se_{e}"))
        self.cnt = {e: 0 for e in ("t", "v", "s", "g")}
        self.pools = {}
        for q, n in zip(("y", "s", "g"), n_dma):
            ps = []
            for i in range(n):
                h = stack.enter_context(nc.semaphore(f"dq_{q}{i}"))
                ps.append(DmaSem(f"dq_{q}{i}", h))
                self.sem[f"dq_{q}{i}"] = h
            self.pools[q] = (ps, [0])
        self.waited = {e: {} for e in ENGS}
        self.final_dma_tokens = {}

    def _waits(self, e, need):
        out = []
        for sk, v in need.items():
            if self.waited[e].get(sk, 0) >= v:
                continue
            self.waited[e][sk] = v
            out.append((self.sem[sk], v))
        return out

    def _need(self, rd, wr):
        need = {}
        def add(tok):
            if tok is None:
                return
            sk, v = tok
            if need.get(sk, 0) < v:
                need[sk] = v
        for b in rd:
            add(b.w)
        for b in wr:
            add(b.w)
            for t_ in b.r:
                add(t_)
        return need

    def op(self, e, emit, rd=(), wr=()):
        need = self._need(rd, wr)
        waits = self._waits(e, need)
        self.cnt[e] += 1
        tok = (e, self.cnt[e])
        sem = self.sem[e]
        val = self.cnt[e]
        def run(eng, waits=waits, emit=emit, sem=sem):
            for h, v in waits:
                eng.wait_ge(h, v)
            emit(eng).then_inc(sem, 1)
        self.body[e].append(run)
        for b in wr:
            b.w = tok
            b.r = []
        for b in rd:
            b.r.append(tok)
        return tok

    def dma(self, q, out_ap, in_ap, rd=(), wr=(), transpose=False):
        pool, idx = self.pools[q]
        s = pool[idx[0] % len(pool)]
        idx[0] += 1
        need = self._need(rd, wr)
        if need.get(s.key, 0) < 16 * s.uses:
            need[s.key] = 16 * s.uses
        waits = self._waits(q if q != "s" else "s", need)
        s.uses += 1
        tok = (s.key, 16 * s.uses)
        def run(eng, waits=waits, o=out_ap, i=in_ap, h=s.handle, tr=transpose):
            for hh, v in waits:
                eng.wait_ge(hh, v)
            if tr:
                eng.dma_start_transpose(o, i).then_inc(h, 16)
            else:
                eng.dma_start(o, i).then_inc(h, 16)
        self.body[{"y": "y", "s": "s", "g": "g"}[q]].append(run)
        for b in wr:
            b.w = tok
            b.r = []
        for b in rd:
            b.r.append(tok)
        self.final_dma_tokens[s.key] = 16 * s.uses
        return tok

    def finish(self):
        waits = []
        for sk, v in self.final_dma_tokens.items():
            waits.append((self.sem[sk], v))
        for e in ("t", "v", "s", "g"):
            if self.cnt[e]:
                waits.append((self.sem[e], self.cnt[e]))
        def run(eng, waits=waits):
            for h, v in waits:
                eng.wait_ge(h, v)
        self.body["y"].append(run)

    def build(self):
        nc = self.nc
        with nc.Block() as blk:
            @blk.tensor
            def _(t):
                for f in self.body["t"]:
                    f(t)
            @blk.vector
            def _(v):
                for f in self.body["v"]:
                    f(v)
            @blk.scalar
            def _(s):
                for f in self.body["s"]:
                    f(s)
            @blk.sync
            def _(y):
                for f in self.body["y"]:
                    f(y)
            @blk.gpsimd
            def _(g):
                for f in self.body["g"]:
                    f(g)

class Region:
    __slots__ = ("mem", "name", "off", "ncols_pad", "dtype", "cols", "ap", "bufs", "nsub")
    def sub(self, i):
        return self.bufs[i]
    def all_bufs(self):
        return self.bufs
    def view(self, c0, c1):
        return self.mem.arena[self.dtype][:, self.off + c0: self.off + c1]

class Mem:
    def __init__(self, nc, sizes):
        self.arena = {}
        self.free_list = {}
        for dt_, cols in sizes.items():
            nm = {F32: "f32", BF16: "bf16", I32: "i32"}[dt_]
            self.arena[dt_] = nc.alloc_sbuf_tensor(f"arena_{nm}", [128, cols], dt_).ap()
            self.free_list[dt_] = [(0, cols, [])]

    def alloc(self, name, dtype, cols, nsub=1):
        ncols = ((cols + 31) // 32) * 32
        fl = self.free_list[dtype]
        small = ncols <= 192
        idxs = range(len(fl) - 1, -1, -1) if small else range(len(fl))
        for i in idxs:
            off, size, toks = fl[i]
            if size >= ncols:
                break
        else:
            used = sum(e[1] for e in fl)
            raise MemoryError(f"SBUF arena({dtype}) OOM for {name}: need {ncols}, free {used}")
        fl.pop(i)
        if small:
            if size > ncols:
                fl.insert(i, (off, size - ncols, list(toks)))
            off = off + size - ncols
        elif size > ncols:
            fl.insert(i, (off + ncols, size - ncols, list(toks)))
        r = Region()
        r.mem, r.name, r.off, r.ncols_pad, r.dtype, r.cols, r.nsub = \
            self, name, off, ncols, dtype, cols, nsub
        r.ap = self.arena[dtype][:, off: off + cols]
        r.bufs = []
        for j in range(nsub):
            b = Buf()
            b.r = list(toks)
            r.bufs.append(b)
        return r

    def free(self, r):
        toks = []
        for b in r.bufs:
            if b.w is not None:
                toks.append(b.w)
            toks.extend(b.r)
        fl = self.free_list[r.dtype]
        fl.append((r.off, r.ncols_pad, toks))
        fl.sort()
        merged = []
        for off, size, tk in fl:
            if merged and merged[-1][0] + merged[-1][1] == off:
                o0, s0, t0 = merged.pop()
                merged.append((o0, s0 + size, t0 + tk))
            else:
                merged.append((off, size, list(tk)))
        self.free_list[r.dtype] = merged


def build_program(nc, cfg):
    NB, S, D = cfg["NB"], cfg["S"], cfg["D"]
    MFEAT, IFFN, H2 = cfg["MFEAT"], cfg["IFFN"], cfg["H2"]
    SC, KD = S // 128, D // 128
    HEADS, HD = D // 64, 64
    MC = MFEAT // 128
    IC, H2C = IFFN // 128, H2 // 128
    H65 = HEADS * 65
    NT_D = D // 512 if D >= 512 else 1
    NTW = min(512, D)

    stack = contextlib.ExitStack()
    p = Prog(nc, stack)
    mem = Mem(nc, {BF16: 79712, F32: 12320, I32: 1024})

    def din(name, shape):
        t = nc.dram_tensor(name, list(shape), F32, kind="ExternalInput")
        return t.ap()
    x_in = din("L0_attn_x", [NB, S, D])
    w0 = {k: din("L0_attn_" + k, [D, D]) for k in ("Wq", "Wk", "Wv", "Wo")}
    b0_ = {k: din("L0_attn_" + k, [D]) for k in ("bq", "bk", "bv", "bo")}
    wg = din("L0_ffn_w_gate", [IFFN, D]); wu = din("L0_ffn_w_up", [IFFN, D])
    wd = din("L0_ffn_w_down", [D, IFFN])
    w2a = {k: din("L1_attn_" + k, [D, D]) for k in ("q_w", "k_w", "v_w", "out_w")}
    b2a = {k: din("L1_attn_" + k, [D]) for k in ("q_b", "k_b", "v_b", "out_b")}
    proj_in = din("L1_attn_proj", [MFEAT, HD])
    gamma_in = din("L1_attn_gamma", [D]); beta_in = din("L1_attn_beta", [D])
    w31 = din("L1_ffn_w1", [H2, D]); b31 = din("L1_ffn_b1", [H2])
    w32 = din("L1_ffn_w2", [D, H2]); b32 = din("L1_ffn_b2", [D])
    cmask_in = nc.dram_tensor("cmask", [128, 128], BF16, kind="ExternalInput").ap()
    cident_in = nc.dram_tensor("cident", [128, 128], BF16, kind="ExternalInput").ap()
    ones_in = nc.dram_tensor("cones", [1, 128], F32, kind="ExternalInput").ap()

    out_d = nc.dram_tensor("out", [NB, S, D], F32, kind="ExternalOutput").ap()
    xln_spill = nc.dram_tensor("xln_spill", [NB, S, D], F32, kind="Internal").ap()
    h_spill = nc.dram_tensor("h_spill", [NB, S, H2], F32, kind="Internal").ap()
    att_spill = nc.dram_tensor("att_spill", [NB, D, S], BF16, kind="Internal").ap()
    wq_gate_d = nc.dram_tensor("wq_gate", [IFFN, D], BF16, kind="Internal").ap()
    wq_up_d = nc.dram_tensor("wq_up", [IFFN, D], BF16, kind="Internal").ap()
    wq_down_d = nc.dram_tensor("wq_down", [D, IFFN], BF16, kind="Internal").ap()
    wq1_d = nc.dram_tensor("wq1", [H2, D], BF16, kind="Internal").ap()
    wq2_d = nc.dram_tensor("wq2", [D, H2], BF16, kind="Internal").ap()

    out_bufs = [[Buf() for _ in range(SC)] for _ in range(NB)]
    xsp_bufs = [[Buf() for _ in range(SC)] for _ in range(NB)]
    hsp_bufs = [[Buf() for _ in range(SC)] for _ in range(NB)]
    asp_bufs = [[Buf() for _ in range(HEADS)] for _ in range(NB)]
    wqd_bufs = {k: Buf() for k in ("g", "u", "d", "1", "2")}

    pb = []
    pb_buf = []
    for i in range(6):
        t = stack.enter_context(nc.psum_tensor(f"bank{i}", [128, 512], F32))
        pb.append(t.ap())
        pb_buf.append(Buf())
    tp = []
    tp_buf = []
    for i in range(2):
        t = stack.enter_context(nc.psum_tensor(f"tbank{i}", [128, 1024], BF16))
        tp.append(t.ap())
        tp_buf.append(Buf())
    tp_ctr = [0]

    cmask = mem.alloc("cmask", BF16, 128)
    cident = mem.alloc("cident", BF16, 128)
    conesf = mem.alloc("ones", F32, 128)
    p.dma("y", cmask.ap, cmask_in, wr=[cmask.sub(0)])
    p.dma("y", cident.ap, cident_in, wr=[cident.sub(0)])
    p.dma("y", conesf.ap[0:1, :], ones_in, wr=[conesf.sub(0)])

    consts = mem.alloc("consts", F32, 4)
    p.op("v", lambda v: v.memset(consts.ap[:, 0:1], 1e-5), wr=[consts.sub(0)])
    p.op("v", lambda v: v.memset(consts.ap[:, 1:2], 1e-6), wr=[consts.sub(0)])
    p.op("v", lambda v: v.memset(consts.ap[:, 2:3], 256.5), wr=[consts.sub(0)])
    EPS5 = consts.ap[:, 0:1]
    EPS6 = consts.ap[:, 1:2]
    C2565 = consts.ap[:, 2:3]

    NSC = NB * SC
    stats = {}
    for nm in ("musum", "sqsum", "mu", "rstd", "var"):
        stats[nm] = mem.alloc(nm, F32, NSC, nsub=1)
    sq_scratch = mem.alloc("sq_scratch", F32, D, nsub=1)

    def emit_stats(tile_ap, tile_buf, col):
        p.op("v", lambda v, a=tile_ap, c=col: v.reduce_sum(
            stats["musum"].ap[:, c:c + 1], a, AX.X),
            rd=[tile_buf], wr=[stats["musum"].sub(0)])
        p.op("s", lambda s, a=tile_ap, c=col: s.activation(
            sq_scratch.ap[:, 0:a.shape[1]], a, AF.Square,
            accum_out=stats["sqsum"].ap[:, c:c + 1]),
            rd=[tile_buf], wr=[stats["sqsum"].sub(0), sq_scratch.sub(0)])

    def finalize_stats(eps):
        st = stats
        p.op("v", lambda v: v.tensor_scalar_mul(st["mu"].ap, st["musum"].ap, 1.0 / D),
             rd=[st["musum"].sub(0)], wr=[st["mu"].sub(0)])
        p.op("v", lambda v: v.tensor_scalar_mul(st["var"].ap, st["sqsum"].ap, 1.0 / D),
             rd=[st["sqsum"].sub(0)], wr=[st["var"].sub(0)])
        p.op("v", lambda v: v.tensor_mul(st["rstd"].ap, st["mu"].ap, st["mu"].ap),
             rd=[st["mu"].sub(0)], wr=[st["rstd"].sub(0)])
        p.op("v", lambda v: v.tensor_sub(st["var"].ap, st["var"].ap, st["rstd"].ap),
             rd=[st["var"].sub(0), st["rstd"].sub(0)], wr=[st["var"].sub(0)])
        p.op("s", lambda s: s.activation(st["rstd"].ap, st["var"].ap, AF.Sqrt, bias=eps),
             rd=[st["var"].sub(0), consts.sub(0)], wr=[st["rstd"].sub(0)])
        p.op("v", lambda v: v.reciprocal(st["rstd"].ap, st["rstd"].ap),
             rd=[st["rstd"].sub(0)], wr=[st["rstd"].sub(0)])

    def load_bias_tile(dst_region, dram_vec, cols, q="g"):
        p.dma(q, dst_region.ap[:, 0:cols],
              dram_vec.unsqueeze(0).partition_broadcast(128).squeeze(1),
              wr=[dst_region.sub(0)])

    def dmat(dst_ap, src_ap, rd, wr, q="y"):
        p.dma(q, dst_ap, src_ap, rd=rd, wr=wr, transpose=True)

    def pet(dst_ap, src_ap, rd, wr):
        i = tp_ctr[0] & 1
        tp_ctr[0] += 1
        p.op("t", lambda t, src_ap=src_ap, i=i: t.transpose(
            tp[i][:, 0:128], src_ap, cident.ap),
            rd=list(rd) + [cident.sub(0)], wr=[tp_buf[i]])
        p.op("s", lambda s, dst_ap=dst_ap, i=i: s.copy(dst_ap, tp[i][:, 0:128]),
             rd=[tp_buf[i]], wr=list(wr))

    xpool = mem.alloc("xpool", F32, 2 * D, nsub=2)
    def xtile(i):
        j = i % 2
        return xpool.view(j * D, (j + 1) * D), xpool.sub(j)

    bg_queue = []

    def bg_pump(k=1):
        for _ in range(min(k, len(bg_queue))):
            bg_queue.pop(0)()

    def bg_flush():
        while bg_queue:
            bg_queue.pop(0)()

    CWMAX = max(D, IFFN)
    wtbg = mem.alloc("wtbg", F32, CWMAX)
    scrbg = mem.alloc("scrbg", F32, 3 * 512 + 2 * (CWMAX // 16))
    def mk_fp4row(src_dram, dst_dram, r0, C, dbuf):
        def task():
            wv = wtbg.view(0, C)
            p.dma("y", wv, src_dram[r0:r0 + 128, :], wr=[wtbg.sub(0)])
            emit_fp4(wv, [wtbg.sub(0)], wv, [wtbg.sub(0)], C, scrbg, CW=512)
            p.dma("g", dst_dram[r0:r0 + 128, :], wv, rd=[wtbg.sub(0)], wr=[dbuf])
        return task
    for (srcd, dstd, R_, C_, key) in ((wg, wq_gate_d, IFFN, D, "g"),
                                      (wu, wq_up_d, IFFN, D, "u"),
                                      (wd, wq_down_d, D, IFFN, "d")):
        for r0 in range(0, R_, 128):
            bg_queue.append(mk_fp4row(srcd, dstd, r0, C_, wqd_bufs[key]))

    onescol = mem.alloc("onescol", F32, 1)
    p.op("v", lambda v: v.memset(onescol.ap, 1.0), wr=[onescol.sub(0)])
    tacc = mem.alloc("tacc", F32, 4)
    swb1 = mem.alloc("swb1", F32, 2)
    swb2 = mem.alloc("swb2", F32, 2)

    def queue_tern_tasks(src_dram, dst_dram, R, C, dbuf, out):
        TCW = min(1024, CWMAX)
        def t_zero():
            p.op("v", lambda v: v.memset(tacc.ap[:, 0:1], 0.0), wr=[tacc.sub(0)])
        bg_queue.append(t_zero)
        def mk_p1(r0, c0, cw):
            def task():
                wv = wtbg.view(0, cw)
                p.dma("y", wv, src_dram[r0:r0 + 128, c0:c0 + cw], wr=[wtbg.sub(0)])
                p.op("v", lambda v, wv=wv: v.tensor_reduce(
                    tacc.ap[:, 1:2], wv, AX.X, ALU.add, apply_absolute_value=True),
                    rd=[wtbg.sub(0)], wr=[tacc.sub(0)])
                p.op("v", lambda v: v.tensor_add(tacc.ap[:, 0:1], tacc.ap[:, 0:1], tacc.ap[:, 1:2]),
                     rd=[tacc.sub(0)], wr=[tacc.sub(0)])
            return task
        for r0 in range(0, R, 128):
            for c0 in range(0, C, TCW):
                bg_queue.append(mk_p1(r0, c0, min(TCW, C - c0)))
        def t_mid():
            p.op("t", lambda t: t.matmul(pb[0][0:1, 0:1], onescol.ap, tacc.ap[:, 0:1],
                                         start=True, stop=True, skip_group_check=True),
                 rd=[onescol.sub(0), tacc.sub(0)], wr=[pb_buf[0]])
            p.op("v", lambda v, n=R * C: v.tensor_scalar(
                tacc.ap[0:1, 2:3], pb[0][0:1, 0:1], 1.0 / n, 1e-5, ALU.mult, ALU.max),
                rd=[pb_buf[0]], wr=[tacc.sub(0)])
            p.op("v", lambda v: v.reciprocal(tacc.ap[0:1, 3:4], tacc.ap[0:1, 2:3]),
                 rd=[tacc.sub(0)], wr=[tacc.sub(0)])
            p.op("t", lambda t: t.matmul(pb[0][0:128, 0:2], conesf.ap[0:1, 0:128],
                                         tacc.ap[0:1, 2:4], start=True, stop=True,
                                         skip_group_check=True),
                 rd=[conesf.sub(0), tacc.sub(0)], wr=[pb_buf[0]])
            p.op("v", lambda v: v.tensor_copy(out.ap, pb[0][0:128, 0:2]),
                 rd=[pb_buf[0]], wr=[out.sub(0)])
        bg_queue.append(t_mid)
        def mk_p2(r0, c0, cw):
            def task():
                wv = wtbg.view(0, cw)
                sv = scrbg.view(0, cw)
                p.dma("y", wv, src_dram[r0:r0 + 128, c0:c0 + cw], wr=[wtbg.sub(0)])
                p.op("v", lambda v, wv=wv: v.tensor_scalar_mul(wv, wv, out.ap[:, 1:2]),
                     rd=[wtbg.sub(0), out.sub(0)], wr=[wtbg.sub(0)])
                p.op("v", lambda v, wv=wv, sv=sv: v.tensor_single_scalar(
                    sv, wv, -0.5, ALU.is_lt), rd=[wtbg.sub(0)], wr=[scrbg.sub(0)])
                p.op("v", lambda v, wv=wv, sv=sv: v.scalar_tensor_tensor(
                    wv, wv, 0.5, sv, ALU.is_gt, ALU.subtract),
                    rd=[wtbg.sub(0), scrbg.sub(0)], wr=[wtbg.sub(0)])
                p.dma("g", dst_dram[r0:r0 + 128, c0:c0 + cw], wv,
                      rd=[wtbg.sub(0)], wr=[dbuf])
            return task
        for r0 in range(0, R, 128):
            for c0 in range(0, C, TCW):
                bg_queue.append(mk_p2(r0, c0, min(TCW, C - c0)))

    for b in range(NB):
        for c in range(SC):
            a, bf = xtile(b * SC + c)
            p.dma("y", a, x_in[b, c * 128:(c + 1) * 128, :], wr=[bf])
            emit_stats(a, bf, b * SC + c)

    def emit_block0():
        finalize_stats(EPS5)
        st = stats
        bias = mem.alloc("b0bias", F32, D)
        wnat = mem.alloc("b0w", BF16, KD * D)
        expT = mem.alloc("expT", BF16, 4 * 512, nsub=4)
        den = mem.alloc("den", F32, 2 * 512, nsub=2)
        for b in range(NB):
            xlnT = mem.alloc("xlnT", BF16, KD * S, nsub=SC)
            xlnbf = mem.alloc("xlnbf", BF16, 2 * D, nsub=2)
            qT = mem.alloc("qT", BF16, KD * S, nsub=KD * SC)
            kT = mem.alloc("kT", BF16, KD * S, nsub=KD * SC)
            vaug = mem.alloc("vaug", BF16, SC * H65, nsub=SC)
            p.op("v", lambda v, a=vaug.ap: v.memset(
                a.rearrange("p (c h e) -> p (c h) e", e=65, h=HEADS)[:, :, 64:65], 1.0),
                wr=list(vaug.all_bufs()))
            for c in range(SC):
                a, bf = xtile(c)
                p.dma("y", a, x_in[b, c * 128:(c + 1) * 128, :], wr=[bf])
                xv = xlnbf.view((c % 2) * D, (c % 2 + 1) * D)
                xvb = xlnbf.sub(c % 2)
                col = b * SC + c
                p.op("v", lambda v, a=a, xv=xv, col=col: v.tensor_scalar(
                    xv, a, st["mu"].ap[:, col:col + 1], st["rstd"].ap[:, col:col + 1],
                    ALU.subtract, ALU.mult),
                    rd=[bf, st["mu"].sub(0), st["rstd"].sub(0)], wr=[xvb])
                for k in range(KD):
                    pet(xlnT.ap[:, k * S + c * 128: k * S + c * 128 + 128],
                        xv[:, k * 128:(k + 1) * 128], rd=[xvb], wr=[xlnT.sub(c)])
            for ti, (wkey, bkey) in enumerate((("Wq", "bq"), ("Wk", "bk"), ("Wv", "bv"))):
                for k in range(KD):
                    p.dma("g", wnat.ap[:, k * D:(k + 1) * D],
                          w0[wkey][k * 128:(k + 1) * 128, :], wr=[wnat.sub(0)])
                load_bias_tile(bias, b0_[bkey], D)
                if ti == 0:
                    p.op("v", lambda v: v.tensor_scalar_mul(bias.ap[:, 0:D], bias.ap[:, 0:D], 0.125),
                         rd=[bias.sub(0)], wr=[bias.sub(0)])
                qsc = 0.125 if ti == 0 else 1.0
                for c in range(SC):
                    dst = None
                    if ti < 2:
                        dst = xlnbf.view((c % 2) * D, (c % 2 + 1) * D)
                        dstb = xlnbf.sub(c % 2)
                    for nh in range(D // NTW):
                        bank = (c % 2) * 2 + (nh % 2)
                        for k in range(KD):
                            p.op("t", lambda t, bank=bank, k=k, c=c, nh=nh, wk=wkey:
                                 t.matmul(pb[bank][:, 0:NTW],
                                          xlnT.ap[:, k * S + c * 128: k * S + c * 128 + 128],
                                          wnat.ap[:, k * D + nh * NTW: k * D + (nh + 1) * NTW],
                                          start=(k == 0), stop=(k == KD - 1)),
                                 rd=[xlnT.sub(c), wnat.sub(0)], wr=[pb_buf[bank]])
                        if ti < 2:
                            p.op("v", lambda v, bank=bank, nh=nh, dst=dst, qsc=qsc:
                                 v.scalar_tensor_tensor(
                                     dst[:, nh * NTW:(nh + 1) * NTW], pb[bank][:, 0:NTW],
                                     qsc, bias.ap[:, nh * NTW:(nh + 1) * NTW],
                                     ALU.mult, ALU.add),
                                 rd=[pb_buf[bank], bias.sub(0)], wr=[dstb])
                        else:
                            va = vaug.ap.rearrange("p (c h e) -> p c h e", e=65, h=HEADS)
                            hpt = NTW // 64
                            p.op("v", lambda v, bank=bank, nh=nh, c=c, va=va, hpt=hpt:
                                 v.scalar_tensor_tensor(
                                     va[:, c, nh * hpt:(nh + 1) * hpt, 0:64],
                                     pb[bank][:, 0:NTW].rearrange("p (h e) -> p h e", e=64),
                                     1.0,
                                     bias.ap[:, nh * NTW:(nh + 1) * NTW].rearrange(
                                         "p (h e) -> p h e", e=64),
                                     ALU.mult, ALU.add),
                                 rd=[pb_buf[bank], bias.sub(0)], wr=[vaug.sub(c)])
                    if ti < 2:
                        tgt = qT if ti == 0 else kT
                        for k in range(KD):
                            pet(tgt.ap[:, k * S + c * 128: k * S + c * 128 + 128],
                                dst[:, k * 128:(k + 1) * 128],
                                rd=[dstb], wr=[tgt.sub(k * SC + c)])
            mem.free(xlnT)
            aoT = mem.alloc("aoT", BF16, KD * S, nsub=KD * SC)
            NSQ = 512 if S >= 512 else S
            nsq_t = S // NSQ
            skpt = NSQ // 128
            LA = 4
            for h in range(HEADS):
                fc, half = h // 2, h % 2
                rows = slice(half * 64, half * 64 + 64)
                for tq in range(nsq_t):
                    qsub = [qT.sub(fc * SC + i) for i in range(tq * skpt,
                                                              min(SC, (tq + 1) * skpt))]
                    nsk = skpt * (tq + 1)
                    def emit_scores(sk, tq=tq, fc=fc, rows=rows, qsub=qsub):
                        col0 = max(0, sk * 128 - tq * NSQ)
                        bank = 4 + (sk % 2)
                        p.op("t", lambda t, bank=bank, sk=sk, col0=col0, tq=tq, fc=fc, rows=rows:
                             t.matmul(pb[bank][:, col0:NSQ],
                                      kT.ap[rows, fc * S + sk * 128: fc * S + sk * 128 + 128],
                                      qT.ap[rows, fc * S + tq * NSQ + col0: fc * S + (tq + 1) * NSQ],
                                      start=True, stop=True),
                             rd=[kT.sub(fc * SC + sk)] + qsub, wr=[pb_buf[bank]])
                        ev = expT.view((sk % LA) * 512, (sk % LA) * 512 + NSQ)
                        evb = expT.sub(sk % LA)
                        p.op("s", lambda s, bank=bank, col0=col0, ev=ev:
                             s.activation(ev[:, col0:NSQ], pb[bank][:, col0:NSQ], AF.Exp),
                             rd=[pb_buf[bank]], wr=[evb])
                        if sk >= skpt * tq:
                            dcol = sk * 128 - tq * NSQ
                            p.op("v", lambda v, ev=ev, dcol=dcol:
                                 v.tensor_tensor(ev[:, dcol:dcol + 128], ev[:, dcol:dcol + 128],
                                                 cmask.ap, ALU.mult),
                                 rd=[evb, cmask.sub(0)], wr=[evb])
                    par = (h * nsq_t + tq) % 2
                    nbk, bbk = (0, 1) if par == 0 else (2, 3)
                    def emit_num(sk, tq=tq, h=h, nsk=nsk, nbk=nbk):
                        col0 = max(0, sk * 128 - tq * NSQ)
                        ev = expT.view((sk % LA) * 512, (sk % LA) * 512 + NSQ)
                        evb = expT.sub(sk % LA)
                        p.op("t", lambda t, sk=sk, col0=col0, h=h, ev=ev, last=(sk == nsk - 1):
                             t.matmul(pb[nbk][0:65, col0:NSQ],
                                      vaug.ap[:, sk * H65 + h * 65: sk * H65 + h * 65 + 65],
                                      ev[:, col0:NSQ],
                                      start=(sk == 0), stop=last),
                             rd=[vaug.sub(sk), evb], wr=[pb_buf[nbk]])
                    LOOK = LA - 1
                    for sk in range(min(LOOK, nsk)):
                        emit_scores(sk)
                    for sk in range(nsk):
                        if sk + LOOK < nsk:
                            emit_scores(sk + LOOK)
                        emit_num(sk)
                    if (h * nsq_t + tq) % 2 == 0:
                        bg_pump(1)
                    dv = den.view((tq % 2) * 512, (tq % 2) * 512 + NSQ)
                    dvb = den.sub(tq % 2)
                    p.op("v", lambda v, dv=dv, nbk=nbk: v.reciprocal(dv[0:1, :], pb[nbk][64:65, 0:NSQ]),
                         rd=[pb_buf[nbk]], wr=[dvb])
                    p.op("t", lambda t, dv=dv, bbk=bbk: t.matmul(pb[bbk][0:64, 0:NSQ],
                                                        conesf.ap[0:1, 0:64], dv[0:1, :],
                                                        start=True, stop=True),
                         rd=[conesf.sub(0), dvb], wr=[pb_buf[bbk]])
                    nsb = xlnbf.view(par * 1024, par * 1024 + NSQ)
                    nsbb = xlnbf.sub(par)
                    p.op("s", lambda s, nsb=nsb, nbk=nbk: s.copy(nsb[0:64, :], pb[nbk][0:64, 0:NSQ]),
                         rd=[pb_buf[nbk]], wr=[nsbb])
                    p.op("v", lambda v, fc=fc, rows=rows, tq=tq, nsb=nsb, bbk=bbk:
                         v.tensor_tensor(aoT.ap[rows, fc * S + tq * NSQ: fc * S + (tq + 1) * NSQ],
                                         nsb[0:64, :], pb[bbk][0:64, 0:NSQ], ALU.mult),
                         rd=[nsbb, pb_buf[bbk]],
                         wr=[aoT.sub(fc * SC + tq * skpt + j) for j in range(skpt)])
            mem.free(qT); mem.free(kT); mem.free(vaug)
            for k in range(KD):
                p.dma("g", wnat.ap[:, k * D:(k + 1) * D],
                      w0["Wo"][k * 128:(k + 1) * 128, :], wr=[wnat.sub(0)])
            load_bias_tile(bias, b0_["bo"], D)
            res = mem.alloc("res0", F32, 2 * D, nsub=2)
            for c in range(SC):
                a, bf = xtile(c)
                p.dma("y", a, x_in[b, c * 128:(c + 1) * 128, :], wr=[bf])
                for nh in range(D // NTW):
                    bank = (c % 2) * 2 + (nh % 2)
                    for k in range(KD):
                        p.op("t", lambda t, bank=bank, k=k, c=c, nh=nh:
                             t.matmul(pb[bank][:, 0:NTW],
                                      aoT.ap[:, k * S + c * 128: k * S + c * 128 + 128],
                                      wnat.ap[:, k * D + nh * NTW: k * D + (nh + 1) * NTW],
                                      start=(k == 0), stop=(k == KD - 1)),
                             rd=[aoT.sub(k * SC + c), wnat.sub(0)], wr=[pb_buf[bank]])
                rv = res.view((c % 2) * D, (c % 2) * D + D)
                rvb = res.sub(c % 2)
                for nh in range(D // NTW):
                    bank = (c % 2) * 2 + (nh % 2)
                    p.op("v", lambda v, bank=bank, nh=nh, rv=rv:
                         v.scalar_tensor_tensor(rv[:, nh * NTW:(nh + 1) * NTW],
                                                pb[bank][:, 0:NTW], 1.0,
                                                bias.ap[:, nh * NTW:(nh + 1) * NTW],
                                                ALU.mult, ALU.add),
                         rd=[pb_buf[bank], bias.sub(0)], wr=[rvb])
                p.op("v", lambda v, rv=rv, a=a: v.tensor_tensor(rv, rv, a, ALU.add),
                     rd=[rvb, bf], wr=[rvb])
                emit_stats(rv, rvb, b * SC + c)
                p.dma("g", out_d[b, c * 128:(c + 1) * 128, :], rv,
                      rd=[rvb], wr=[out_bufs[b][c]])
            mem.free(aoT); mem.free(res); mem.free(xlnbf)
        mem.free(bias); mem.free(wnat); mem.free(expT); mem.free(den)

    FP4CW = 1024

    def emit_fp4(x_ap, x_bufs, dst_ap, dst_bufs, W, scr, CW=None):
        FP4CW = CW or 1024
        nb_all = W // 16
        sb = scr.sub(0)
        for c0 in range(0, W, FP4CW):
            c1 = min(W, c0 + FP4CW)
            cw = c1 - c0
            nb = cw // 16
            x3 = x_ap[:, c0:c1].rearrange("p (n k) -> p n k", k=16)
            bmv = scr.view(3 * FP4CW, 3 * FP4CW + nb_all)[:, c0 // 16: c1 // 16]
            rcpv = scr.view(3 * FP4CW + nb_all, 3 * FP4CW + 2 * nb_all)[:, c0 // 16: c1 // 16]
            a = scr.view(0, cw)
            q = scr.view(FP4CW, FP4CW + cw)
            bb = scr.view(2 * FP4CW, 2 * FP4CW + cw)
            bm3 = bmv.unsqueeze(2).broadcast_to([128, nb, 16])
            rcp3 = rcpv.unsqueeze(2).broadcast_to([128, nb, 16])
            d2 = dst_ap[:, c0:c1]
            rw = list(x_bufs) + [sb]
            p.op("v", lambda v, x3=x3, bmv=bmv: v.tensor_reduce(
                bmv.unsqueeze(2), x3, AX.X, ALU.max, apply_absolute_value=True),
                rd=list(x_bufs), wr=[sb])
            p.op("v", lambda v, bmv=bmv: v.tensor_scalar(
                bmv, bmv, 1.0 / 6.0, 1e-12, ALU.mult, ALU.max), rd=[sb], wr=[sb])
            p.op("v", lambda v, bmv=bmv, rcpv=rcpv: v.reciprocal(rcpv, bmv),
                 rd=[sb], wr=[sb])
            p.op("v", lambda v, x3=x3, rcp3=rcp3: v.tensor_tensor(
                x3, x3, rcp3, ALU.mult), rd=rw, wr=list(x_bufs))
            xv = x_ap[:, c0:c1]
            p.op("v", lambda v, a=a, xv=xv: v.scalar_tensor_tensor(
                a, xv, -1.0, xv, ALU.mult, ALU.max), rd=rw, wr=[sb])
            p.op("v", lambda v, a=a, q=q: v.tensor_scalar(
                q, a, 0.25, None, ALU.is_gt), rd=[sb], wr=[sb])
            for thr in (0.75, 1.25, 1.75):
                p.op("v", lambda v, a=a, q=q, thr=thr: v.scalar_tensor_tensor(
                    q, a, thr, q, ALU.is_gt, ALU.add), rd=[sb], wr=[sb])
            p.op("v", lambda v, a=a, bb=bb: v.tensor_scalar(
                bb, a, 2.5, None, ALU.is_gt), rd=[sb], wr=[sb])
            p.op("v", lambda v, a=a, bb=bb: v.scalar_tensor_tensor(
                bb, a, 3.5, bb, ALU.is_gt, ALU.add), rd=[sb], wr=[sb])
            p.op("v", lambda v, q=q, bb=bb: v.scalar_tensor_tensor(
                q, q, 0.5, bb, ALU.mult, ALU.add), rd=[sb], wr=[sb])
            p.op("v", lambda v, a=a, bb=bb: v.tensor_scalar(
                bb, a, 5.0, 2.0, ALU.is_gt, ALU.mult), rd=[sb], wr=[sb])
            p.op("v", lambda v, q=q, bb=bb: v.tensor_add(q, q, bb), rd=[sb], wr=[sb])
            p.op("s", lambda s, a=a, xv=xv: s.sign(a, xv), rd=rw, wr=[sb])
            p.op("v", lambda v, a=a, bm3=bm3, cw=cw: v.tensor_tensor(
                a.rearrange("p (n k) -> p n k", k=16), a.rearrange("p (n k) -> p n k", k=16),
                bm3, ALU.mult), rd=[sb], wr=[sb])
            p.op("v", lambda v, q=q, a=a, d2=d2: v.tensor_tensor(d2, q, a, ALU.mult),
                 rd=[sb], wr=list(dst_bufs))

    def quant_weight_fp4(src_dram, dst_dram, R, C, dbuf):
        wt = mem.alloc("wqt", F32, 2 * C, nsub=2)
        qt = mem.alloc("wqq", BF16, 2 * C, nsub=2)
        scr = mem.alloc("wqscr", F32, 3 * FP4CW + 2 * (C // 16))
        for j, r0 in enumerate(range(0, R, 128)):
            wv = wt.view((j % 2) * C, (j % 2) * C + C)
            qv = qt.view((j % 2) * C, (j % 2) * C + C)
            p.dma("y", wv, src_dram[r0:r0 + 128, :], wr=[wt.sub(j % 2)])
            emit_fp4(wv, [wt.sub(j % 2)], qv, [qt.sub(j % 2)], C, scr)
            p.dma("g", dst_dram[r0:r0 + 128, :], qv, rd=[qt.sub(j % 2)], wr=[dbuf])
        mem.free(wt); mem.free(qt); mem.free(scr)

    def emit_block1():
        nonlocal wtbg, scrbg
        bg_flush()
        mem.free(wtbg); mem.free(scrbg)
        finalize_stats(EPS5)
        NGU = max(1, (KD * IFFN) // 512)
        SUBW = (KD * IFFN) // NGU
        wgT = mem.alloc("wgT", BF16, KD * IFFN, nsub=NGU)
        wuT = mem.alloc("wuT", BF16, KD * IFFN, nsub=NGU)
        wdT = mem.alloc("wdT", BF16, IC * D)
        ntiles = [(n0, min(512, IFFN - n0)) for n0 in range(0, IFFN, 512)]
        for n0, nw in ntiles:
            for k in range(KD):
                for i in range(n0 // 128, (n0 + nw) // 128):
                    cc = k * IFFN + i * 128
                    dmat(wgT.ap[:, cc:cc + 128],
                         wq_gate_d[i * 128:(i + 1) * 128, k * 128:(k + 1) * 128],
                         rd=[wqd_bufs["g"]], wr=[wgT.sub(cc // SUBW)])
                    dmat(wuT.ap[:, cc:cc + 128],
                         wq_up_d[i * 128:(i + 1) * 128, k * 128:(k + 1) * 128],
                         rd=[wqd_bufs["u"]], wr=[wuT.sub(cc // SUBW)])
        for i in range(IC):
            for k in range(KD):
                dmat(wdT.ap[:, i * D + k * 128: i * D + (k + 1) * 128],
                     wq_down_d[k * 128:(k + 1) * 128, i * 128:(i + 1) * 128],
                     rd=[wqd_bufs["d"]], wr=[wdT.sub(0)])
        xq = mem.alloc("xq", BF16, D)
        xqT = mem.alloc("xqT", BF16, 2 * KD * 128, nsub=2)
        hqc = mem.alloc("hq", BF16, IFFN)
        hqT = mem.alloc("hqT", BF16, IC * 128)
        scr = mem.alloc("b1scr", F32, 3 * FP4CW + 2 * (max(D, IFFN) // 16))
        xlnf = mem.alloc("xlnf", F32, D)
        hf = mem.alloc("hf", F32, 2 * 512, nsub=2)
        sil = mem.alloc("sil", F32, 512)
        res = mem.alloc("res1", F32, 2 * D, nsub=2)
        dtiles = [(n0, min(512, D - n0)) for n0 in range(0, D, 512)]
        st = stats
        for b in range(NB):
            for c in range(SC):
                a, bf = xtile(c)
                p.dma("y", a, out_d[b, c * 128:(c + 1) * 128, :],
                      rd=[out_bufs[b][c]], wr=[bf])
                col = b * SC + c
                p.op("v", lambda v, a=a, col=col: v.tensor_scalar(
                    xlnf.ap, a, st["mu"].ap[:, col:col + 1], st["rstd"].ap[:, col:col + 1],
                    ALU.subtract, ALU.mult),
                    rd=[bf, st["mu"].sub(0), st["rstd"].sub(0)], wr=[xlnf.sub(0)])
                emit_fp4(xlnf.ap, [xlnf.sub(0)], xq.ap, [xq.sub(0)], D, scr)
                xqTo = (c % 2) * KD * 128
                for k in range(KD):
                    pet(xqT.ap[:, xqTo + k * 128: xqTo + (k + 1) * 128],
                        xq.ap[:, k * 128:(k + 1) * 128],
                        rd=[xq.sub(0)], wr=[xqT.sub(c % 2)])
                for nt, (n0, nw) in enumerate(ntiles):
                    bg, bu = nt % 2, 2 + nt % 2
                    for k in range(KD):
                        p.op("t", lambda t, bg=bg, k=k, n0=n0, nw=nw, xqTo=xqTo: t.matmul(
                            pb[bg][:, 0:nw], xqT.ap[:, xqTo + k * 128: xqTo + (k + 1) * 128],
                            wgT.ap[:, k * IFFN + n0: k * IFFN + n0 + nw],
                            start=(k == 0), stop=(k == KD - 1)),
                            rd=[xqT.sub(c % 2)] + [wgT.sub(s) for s in
                                range((k * IFFN + n0) // SUBW,
                                      (k * IFFN + n0 + nw - 1) // SUBW + 1)],
                            wr=[pb_buf[bg]])
                    for k in range(KD):
                        p.op("t", lambda t, bu=bu, k=k, n0=n0, nw=nw, xqTo=xqTo: t.matmul(
                            pb[bu][:, 0:nw], xqT.ap[:, xqTo + k * 128: xqTo + (k + 1) * 128],
                            wuT.ap[:, k * IFFN + n0: k * IFFN + n0 + nw],
                            start=(k == 0), stop=(k == KD - 1)),
                            rd=[xqT.sub(c % 2)] + [wuT.sub(s) for s in
                                range((k * IFFN + n0) // SUBW,
                                      (k * IFFN + n0 + nw - 1) // SUBW + 1)],
                            wr=[pb_buf[bu]])
                    p.op("s", lambda s, bg=bg, nw=nw: s.activation(
                        sil.ap[:, 0:nw], pb[bg][:, 0:nw], AF.Silu),
                        rd=[pb_buf[bg]], wr=[sil.sub(0)])
                    hv = hf.view((nt % 2) * 512, (nt % 2) * 512 + nw)
                    hvb = hf.sub(nt % 2)
                    p.op("v", lambda v, hv=hv, bu=bu, nw=nw: v.tensor_tensor(
                        hv, sil.ap[:, 0:nw], pb[bu][:, 0:nw], ALU.mult),
                        rd=[sil.sub(0), pb_buf[bu]], wr=[hvb])
                    emit_fp4(hv, [hvb], hqc.ap[:, n0:n0 + nw], [hqc.sub(0)], nw, scr)
                for i in range(IC):
                    pet(hqT.ap[:, i * 128:(i + 1) * 128], hqc.ap[:, i * 128:(i + 1) * 128],
                        rd=[hqc.sub(0)], wr=[hqT.sub(0)])
                rv = res.view((c % 2) * D, (c % 2) * D + D)
                rvb = res.sub(c % 2)
                for dn, (n0, nw) in enumerate(dtiles):
                    bank = 4 + dn % 2
                    for i in range(IC):
                        p.op("t", lambda t, bank=bank, i=i, n0=n0, nw=nw: t.matmul(
                            pb[bank][:, 0:nw], hqT.ap[:, i * 128:(i + 1) * 128],
                            wdT.ap[:, i * D + n0: i * D + n0 + nw],
                            start=(i == 0), stop=(i == IC - 1)),
                            rd=[hqT.sub(0), wdT.sub(0)], wr=[pb_buf[bank]])
                    p.op("v", lambda v, bank=bank, n0=n0, nw=nw, rv=rv, a=a:
                         v.tensor_tensor(rv[:, n0:n0 + nw], pb[bank][:, 0:nw],
                                         a[:, n0:n0 + nw], ALU.add),
                         rd=[pb_buf[bank], bf], wr=[rvb])
                emit_stats(rv, rvb, b * SC + c)
                p.dma("g", out_d[b, c * 128:(c + 1) * 128, :], rv,
                      rd=[rvb], wr=[out_bufs[b][c]])
        for r in (wgT, wuT, wdT, xq, xqT, hqc, hqT, scr, xlnf, hf, sil, res):
            mem.free(r)

    def emit_block2():
        nonlocal wtbg, scrbg
        wtbg = mem.alloc("wtbg2", F32, 1024)
        scrbg = mem.alloc("scrbg2", F32, 1024)
        queue_tern_tasks(w31, wq1_d, H2, D, wqd_bufs["1"], swb1)
        queue_tern_tasks(w32, wq2_d, D, H2, wqd_bufs["2"], swb2)
        finalize_stats(EPS5)
        st = stats
        projT = mem.alloc("projT", BF16, MC * 128)
        ptmp = mem.alloc("ptmp", BF16, 128)
        for mc in range(MC):
            p.dma("g", ptmp.ap[:, 0:64], proj_in[mc * 128:(mc + 1) * 128, :],
                  wr=[ptmp.sub(0)])
            p.dma("g", ptmp.ap[:, 64:128], proj_in[mc * 128:(mc + 1) * 128, :],
                  rd=[ptmp.sub(0)], wr=[ptmp.sub(0)])
            dmat(projT.ap[:, mc * 128:(mc + 1) * 128], ptmp.ap,
                 rd=[ptmp.sub(0)], wr=[projT.sub(0)])
        mem.free(ptmp)
        ones128 = mem.alloc("ones128", BF16, 1)
        p.op("v", lambda v: v.memset(ones128.ap, 1.0), wr=[ones128.sub(0)])
        gam = mem.alloc("gam", F32, D)
        bet = mem.alloc("bet", F32, D)
        load_bias_tile(gam, gamma_in, D)
        load_bias_tile(bet, beta_in, D)
        bias = mem.alloc("b2bias", F32, D)
        b2sq = mem.alloc("b2sq", F32, D)
        inner = mem.alloc("inner", F32, 8)
        qptmp = mem.alloc("qptmp", BF16, 2 * MFEAT, nsub=2)
        den2 = mem.alloc("den2", F32, 2 * 128, nsub=2)
        aoTp = mem.alloc("aoTp", BF16, S, nsub=SC)
        for b in range(NB):
            wnat = mem.alloc("b2w", BF16, KD * D)
            xlnT = mem.alloc("xlnT2", BF16, KD * S, nsub=SC)
            xlnbf = mem.alloc("xlnbf2", BF16, 2 * D, nsub=2)
            xlnf = mem.alloc("xlnf2", F32, D)
            qTa = mem.alloc("qTa", BF16, KD * S, nsub=KD * SC)
            kTa = mem.alloc("kTa", BF16, KD * S, nsub=KD * SC)
            vnat = mem.alloc("vnat", BF16, SC * D, nsub=SC)
            ncq = mem.alloc("ncq", F32, SC * HEADS)
            nck = mem.alloc("nck", F32, SC * HEADS)
            for c in range(SC):
                a, bf = xtile(c)
                p.dma("y", a, out_d[b, c * 128:(c + 1) * 128, :],
                      rd=[out_bufs[b][c]], wr=[bf])
                col = b * SC + c
                p.op("v", lambda v, a=a, col=col: v.tensor_scalar(
                    xlnf.ap, a, st["mu"].ap[:, col:col + 1], st["rstd"].ap[:, col:col + 1],
                    ALU.subtract, ALU.mult),
                    rd=[bf, st["mu"].sub(0), st["rstd"].sub(0)], wr=[xlnf.sub(0)])
                p.dma("g", xln_spill[b, c * 128:(c + 1) * 128, :], xlnf.ap,
                      rd=[xlnf.sub(0)], wr=[xsp_bufs[b][c]])
                xv = xlnbf.view((c % 2) * D, (c % 2 + 1) * D)
                xvb = xlnbf.sub(c % 2)
                p.op("v", lambda v, xv=xv: v.tensor_copy(xv, xlnf.ap),
                     rd=[xlnf.sub(0)], wr=[xvb])
                for k in range(KD):
                    pet(xlnT.ap[:, k * S + c * 128: k * S + c * 128 + 128],
                        xv[:, k * 128:(k + 1) * 128], rd=[xvb], wr=[xlnT.sub(c)])
            for ti, (wkey, bkey) in enumerate(
                    (("q_w", "q_b"), ("k_w", "k_b"), ("v_w", "v_b"))):
                for k in range(KD):
                    p.dma("g", wnat.ap[:, k * D:(k + 1) * D],
                          w2a[wkey][k * 128:(k + 1) * 128, :], wr=[wnat.sub(0)])
                load_bias_tile(bias, b2a[bkey], D)
                if ti < 2:
                    p.op("v", lambda v: v.tensor_scalar_mul(bias.ap[:, 0:D], bias.ap[:, 0:D], 0.125),
                         rd=[bias.sub(0)], wr=[bias.sub(0)])
                qsc = 0.125 if ti < 2 else 1.0
                tgt = (qTa, kTa, None)[ti]
                ncd = (ncq, nck, None)[ti]
                for c in range(SC):
                    if ti < 2:
                        dst = xlnbf.view((c % 2) * D, (c % 2 + 1) * D)
                        dstb = xlnbf.sub(c % 2)
                    for nh in range(D // NTW):
                        bank = (c % 2) * 2 + (nh % 2)
                        for k in range(KD):
                            p.op("t", lambda t, bank=bank, k=k, c=c, nh=nh:
                                 t.matmul(pb[bank][:, 0:NTW],
                                          xlnT.ap[:, k * S + c * 128: k * S + c * 128 + 128],
                                          wnat.ap[:, k * D + nh * NTW: k * D + (nh + 1) * NTW],
                                          start=(k == 0), stop=(k == KD - 1)),
                                 rd=[xlnT.sub(c), wnat.sub(0)], wr=[pb_buf[bank]])
                        if ti < 2:
                            p.op("v", lambda v, bank=bank, nh=nh, dst=dst, qsc=qsc:
                                 v.scalar_tensor_tensor(
                                     dst[:, nh * NTW:(nh + 1) * NTW], pb[bank][:, 0:NTW],
                                     qsc, bias.ap[:, nh * NTW:(nh + 1) * NTW],
                                     ALU.mult, ALU.add),
                                 rd=[pb_buf[bank], bias.sub(0)], wr=[dstb])
                        else:
                            p.op("v", lambda v, bank=bank, nh=nh, c=c:
                                 v.scalar_tensor_tensor(
                                     vnat.ap[:, c * D + nh * NTW: c * D + (nh + 1) * NTW],
                                     pb[bank][:, 0:NTW], 1.0,
                                     bias.ap[:, nh * NTW:(nh + 1) * NTW], ALU.mult, ALU.add),
                                 rd=[pb_buf[bank], bias.sub(0)], wr=[vnat.sub(c)])
                    if ti < 2:
                        p.op("v", lambda v, dst=dst: v.tensor_mul(b2sq.ap, dst, dst),
                             rd=[dstb], wr=[b2sq.sub(0)])
                        p.op("v", lambda v, c=c, ncd=ncd: v.tensor_reduce(
                            ncd.ap[:, c * HEADS:(c + 1) * HEADS].unsqueeze(2),
                            b2sq.ap.rearrange("p (h e) -> p h e", e=64), AX.X, ALU.add),
                            rd=[b2sq.sub(0)], wr=[ncd.sub(0)])
                        p.op("v", lambda v, c=c, ncd=ncd: v.tensor_scalar_mul(
                            ncd.ap[:, c * HEADS:(c + 1) * HEADS],
                            ncd.ap[:, c * HEADS:(c + 1) * HEADS], -0.5),
                            rd=[ncd.sub(0)], wr=[ncd.sub(0)])
                        for k in range(KD):
                            pet(tgt.ap[:, k * S + c * 128: k * S + c * 128 + 128],
                                dst[:, k * 128:(k + 1) * 128],
                                rd=[dstb], wr=[tgt.sub(k * SC + c)])
                    bg_pump(1)
            mem.free(xlnT); mem.free(xlnbf); mem.free(xlnf); mem.free(wnat)
            hsets = []
            for hl in range(2):
                hsets.append(dict(
                    qpT=mem.alloc(f"qpT{hl}", BF16, MC * S),
                    kpT=mem.alloc(f"kpT{hl}", BF16, MC * S),
                    kpn=mem.alloc(f"kpn{hl}", BF16, SC * MFEAT, nsub=SC),
                    KVs=mem.alloc(f"KVs{hl}", BF16, MC * 65),
                    KVf=mem.alloc(f"KVf{hl}", F32, MC * 65),
                    aT=mem.alloc(f"aT{hl}", BF16, 128),
                ))
            NBANK = (5, 1)
            ABANK = (2, 3)
            KVOFF = (0, 130)
            for fc in range(HEADS // 2):
                for hl in range(2):
                    hs = hsets[hl]
                    h = 2 * fc + hl
                    rows = slice(hl * 64, hl * 64 + 64)
                    p.op("v", lambda v, hs=hs: v.memset(hs["KVf"].ap, 0.0),
                         wr=[hs["KVf"].sub(0)])
                    for ti2, (srcT, ncd) in enumerate(((qTa, ncq), (kTa, nck))):
                        for c in range(SC):
                            bank = c % 2
                            p.op("t", lambda t, bank=bank, c=c, srcT=srcT, rows=rows, fc=fc:
                                 t.matmul(pb[bank][:, 0:MFEAT],
                                          srcT.ap[rows, fc * S + c * 128: fc * S + (c + 1) * 128],
                                          projT.ap[rows, :],
                                          start=True, stop=True),
                                 rd=[srcT.sub(fc * SC + c), projT.sub(0)], wr=[pb_buf[bank]])
                            if ti2 == 0:
                                qv = qptmp.view((c % 2) * MFEAT, (c % 2) * MFEAT + MFEAT)
                                qvb = qptmp.sub(c % 2)
                                p.op("s", lambda s, bank=bank, c=c, qv=qv, ncd=ncd, h=h:
                                     s.activation(qv, pb[bank][:, 0:MFEAT], AF.Exp,
                                                  bias=ncd.ap[:, c * HEADS + h: c * HEADS + h + 1]),
                                     rd=[pb_buf[bank], ncd.sub(0)], wr=[qvb])
                                for mc in range(MC):
                                    pet(hs["qpT"].ap[:, mc * S + c * 128: mc * S + (c + 1) * 128],
                                        qv[:, mc * 128:(mc + 1) * 128],
                                        rd=[qvb], wr=[hs["qpT"].sub(0)])
                            else:
                                kslice = hs["kpn"].ap[:, c * MFEAT:(c + 1) * MFEAT]
                                p.op("s", lambda s, bank=bank, c=c, kslice=kslice, ncd=ncd, h=h:
                                     s.activation(kslice, pb[bank][:, 0:MFEAT], AF.Exp,
                                                  bias=ncd.ap[:, c * HEADS + h: c * HEADS + h + 1]),
                                     rd=[pb_buf[bank], ncd.sub(0)], wr=[hs["kpn"].sub(c)])
                                for mc in range(MC):
                                    pet(hs["kpT"].ap[:, mc * S + c * 128: mc * S + (c + 1) * 128],
                                        kslice[:, mc * 128:(mc + 1) * 128],
                                        rd=[hs["kpn"].sub(c)], wr=[hs["kpT"].sub(0)])
                for ci in range(SC):
                    for hl in range(2):
                        hs = hsets[hl]
                        h = 2 * fc + hl
                        rows = slice(hl * 64, hl * 64 + 64)
                        nb_, ab, kvo = NBANK[hl], ABANK[hl], KVOFF[hl]
                        qpT, kpT, kpn = hs["qpT"], hs["kpT"], hs["kpn"]
                        KVs, KVf, aT = hs["KVs"], hs["KVf"], hs["aT"]
                        vs = vnat.ap[:, ci * D + fc * 128 + hl * 64:
                                     ci * D + fc * 128 + hl * 64 + 64]
                        first = ci == 0
                        if not first:
                            for mc in range(MC):
                                p.op("t", lambda t, mc=mc, ci=ci, nb_=nb_, qpT=qpT, KVs=KVs: t.matmul(
                                    pb[nb_][0:64, 0:128],
                                    KVs.ap[:, mc * 65: mc * 65 + 64],
                                    qpT.ap[:, mc * S + ci * 128: mc * S + (ci + 1) * 128],
                                    start=(mc == 0), stop=False, skip_group_check=True),
                                    rd=[KVs.sub(0), qpT.sub(0)], wr=[pb_buf[nb_]])
                                p.op("t", lambda t, mc=mc, ci=ci, nb_=nb_, qpT=qpT, KVs=KVs: t.matmul(
                                    pb[nb_][64:65, 0:128],
                                    KVs.ap[:, mc * 65 + 64: mc * 65 + 65],
                                    qpT.ap[:, mc * S + ci * 128: mc * S + (ci + 1) * 128],
                                    start=(mc == 0), stop=False, skip_group_check=True),
                                    rd=[KVs.sub(0), qpT.sub(0)], wr=[pb_buf[nb_]])
                        for mc in range(MC):
                            p.op("t", lambda t, mc=mc, ci=ci, ab=ab, kpT=kpT, qpT=qpT: t.matmul(
                                pb[ab][:, 0:128],
                                kpT.ap[:, mc * S + ci * 128: mc * S + (ci + 1) * 128],
                                qpT.ap[:, mc * S + ci * 128: mc * S + (ci + 1) * 128],
                                start=(mc == 0), stop=(mc == MC - 1)),
                                rd=[kpT.sub(0), qpT.sub(0)], wr=[pb_buf[ab]])
                        p.op("v", lambda v, ab=ab, aT=aT: v.tensor_tensor(
                            aT.ap, pb[ab][:, 0:128], cmask.ap, ALU.mult),
                            rd=[pb_buf[ab], cmask.sub(0)], wr=[aT.sub(0)])
                        p.op("t", lambda t, vs=vs, first=first, nb_=nb_, aT=aT: t.matmul(
                            pb[nb_][0:64, 0:128], vs, aT.ap,
                            start=first, stop=True, skip_group_check=True),
                            rd=[vnat.sub(ci), aT.sub(0)], wr=[pb_buf[nb_]])
                        p.op("t", lambda t, first=first, nb_=nb_, aT=aT: t.matmul(
                            pb[nb_][64:65, 0:128], ones128.ap, aT.ap,
                            start=first, stop=True, skip_group_check=True),
                            rd=[ones128.sub(0), aT.sub(0)], wr=[pb_buf[nb_]])
                        for mc in range(MC):
                            p.op("t", lambda t, mc=mc, ci=ci, vs=vs, kvo=kvo, kpn=kpn: t.matmul(
                                pb[4][:, kvo + mc * 65: kvo + mc * 65 + 64],
                                kpn.ap[:, ci * MFEAT + mc * 128: ci * MFEAT + (mc + 1) * 128],
                                vs, start=True, stop=True, skip_group_check=True),
                                rd=[kpn.sub(ci), vnat.sub(ci)], wr=[pb_buf[4]])
                            p.op("t", lambda t, mc=mc, ci=ci, kvo=kvo, kpn=kpn: t.matmul(
                                pb[4][:, kvo + mc * 65 + 64: kvo + mc * 65 + 65],
                                kpn.ap[:, ci * MFEAT + mc * 128: ci * MFEAT + (mc + 1) * 128],
                                ones128.ap, start=True, stop=True, skip_group_check=True),
                                rd=[kpn.sub(ci), ones128.sub(0)], wr=[pb_buf[4]])
                        for mc in range(MC):
                            kvv = KVf.view(mc * 65, (mc + 1) * 65)
                            p.op("v", lambda v, mc=mc, kvv=kvv, kvo=kvo: v.tensor_tensor(
                                kvv, kvv, pb[4][:, kvo + mc * 65: kvo + (mc + 1) * 65], ALU.add),
                                rd=[KVf.sub(0), pb_buf[4]], wr=[KVf.sub(0)])
                            p.op("v", lambda v, mc=mc, kvv=kvv, KVs=KVs: v.tensor_copy(
                                KVs.ap[:, mc * 65:(mc + 1) * 65], kvv),
                                rd=[KVf.sub(0)], wr=[KVs.sub(0)])
                        dv = den2.view(hl * 128, hl * 128 + 128)
                        dvb = den2.sub(hl)
                        nsb2 = qptmp.view(hl * MFEAT, hl * MFEAT + 128)
                        nsb2b = qptmp.sub(hl)
                        p.op("v", lambda v, nb_=nb_, dv=dv: v.reciprocal(dv[0:1, :], pb[nb_][64:65, 0:128]),
                             rd=[pb_buf[nb_]], wr=[dvb])
                        p.op("s", lambda s, nb_=nb_, nsb2=nsb2: s.copy(nsb2[0:64, :], pb[nb_][0:64, 0:128]),
                             rd=[pb_buf[nb_]], wr=[nsb2b])
                        p.op("t", lambda t, nb_=nb_, dv=dv: t.matmul(pb[nb_][0:64, 0:128],
                                                     conesf.ap[0:1, 0:64], dv[0:1, :],
                                                     start=True, stop=True, skip_group_check=True),
                             rd=[conesf.sub(0), dvb], wr=[pb_buf[nb_]])
                        p.op("v", lambda v, ci=ci, rows=rows, nb_=nb_, nsb2=nsb2: v.tensor_tensor(
                            aoTp.ap[rows, ci * 128:(ci + 1) * 128],
                            nsb2[0:64, :], pb[nb_][0:64, 0:128], ALU.mult),
                            rd=[nsb2b, pb_buf[nb_]], wr=[aoTp.sub(ci)])
                p.dma("g", att_spill[b, fc * 128:(fc + 1) * 128, :], aoTp.ap,
                      rd=list(aoTp.all_bufs()), wr=[asp_bufs[b][fc]])
                bg_pump(1)
            for hs in hsets:
                for r in hs.values():
                    mem.free(r)
            mem.free(qTa); mem.free(kTa); mem.free(vnat)
            mem.free(ncq); mem.free(nck)
            wnat = mem.alloc("b2w", BF16, KD * D)
            for k in range(KD):
                p.dma("g", wnat.ap[:, k * D:(k + 1) * D],
                      w2a["out_w"][k * 128:(k + 1) * 128, :], wr=[wnat.sub(0)])
            load_bias_tile(bias, b2a["out_b"], D)
            aol = mem.alloc("aol", BF16, 2 * KD * 128, nsub=2)
            res = mem.alloc("res2", F32, 2 * D, nsub=2)
            for c in range(SC):
                alo = (c % 2) * KD * 128
                for k in range(KD):
                    p.dma("y", aol.ap[:, alo + k * 128: alo + (k + 1) * 128],
                          att_spill[b, k * 128:(k + 1) * 128, c * 128:(c + 1) * 128],
                          rd=[asp_bufs[b][k]], wr=[aol.sub(c % 2)])
                for nh in range(D // NTW):
                    bank = (c % 2) * 2 + (nh % 2)
                    for k in range(KD):
                        p.op("t", lambda t, bank=bank, k=k, nh=nh, alo=alo: t.matmul(
                            pb[bank][:, 0:NTW], aol.ap[:, alo + k * 128: alo + (k + 1) * 128],
                            wnat.ap[:, k * D + nh * NTW: k * D + (nh + 1) * NTW],
                            start=(k == 0), stop=(k == KD - 1)),
                            rd=[aol.sub(c % 2), wnat.sub(0)], wr=[pb_buf[bank]])
                rv = res.view((c % 2) * D, (c % 2) * D + D)
                rvb = res.sub(c % 2)
                for nh in range(D // NTW):
                    bank = (c % 2) * 2 + (nh % 2)
                    p.op("v", lambda v, bank=bank, nh=nh, rv=rv: v.scalar_tensor_tensor(
                        rv[:, nh * NTW:(nh + 1) * NTW], pb[bank][:, 0:NTW], 1.0,
                        bias.ap[:, nh * NTW:(nh + 1) * NTW], ALU.mult, ALU.add),
                        rd=[pb_buf[bank], bias.sub(0)], wr=[rvb])
                a, bf = xtile(c)
                p.dma("y", a, xln_spill[b, c * 128:(c + 1) * 128, :],
                      rd=[xsp_bufs[b][c]], wr=[bf])
                p.op("v", lambda v, rv=rv, a=a: v.tensor_tensor(rv, rv, a, ALU.add),
                     rd=[rvb, bf], wr=[rvb])
                p.op("v", lambda v, rv=rv: v.reduce_sum(inner.ap[:, 0:1], rv, AX.X),
                     rd=[rvb], wr=[inner.sub(0)])
                p.op("s", lambda s, rv=rv: s.activation(
                    sq_scratch.ap[:, 0:D], rv, AF.Square, accum_out=inner.ap[:, 1:2]),
                    rd=[rvb], wr=[sq_scratch.sub(0), inner.sub(0)])
                p.op("v", lambda v: v.tensor_scalar_mul(inner.ap[:, 2:3], inner.ap[:, 0:1], 1.0 / D),
                     rd=[inner.sub(0)], wr=[inner.sub(0)])
                p.op("v", lambda v: v.tensor_scalar_mul(inner.ap[:, 3:4], inner.ap[:, 1:2], 1.0 / D),
                     rd=[inner.sub(0)], wr=[inner.sub(0)])
                p.op("v", lambda v: v.tensor_mul(inner.ap[:, 4:5], inner.ap[:, 2:3], inner.ap[:, 2:3]),
                     rd=[inner.sub(0)], wr=[inner.sub(0)])
                p.op("v", lambda v: v.tensor_sub(inner.ap[:, 3:4], inner.ap[:, 3:4], inner.ap[:, 4:5]),
                     rd=[inner.sub(0)], wr=[inner.sub(0)])
                p.op("s", lambda s: s.activation(inner.ap[:, 5:6], inner.ap[:, 3:4],
                                                 AF.Sqrt, bias=EPS6),
                     rd=[inner.sub(0), consts.sub(0)], wr=[inner.sub(0)])
                p.op("v", lambda v: v.reciprocal(inner.ap[:, 5:6], inner.ap[:, 5:6]),
                     rd=[inner.sub(0)], wr=[inner.sub(0)])
                p.op("v", lambda v, rv=rv: v.tensor_scalar(
                    rv, rv, inner.ap[:, 2:3], inner.ap[:, 5:6], ALU.subtract, ALU.mult),
                    rd=[rvb, inner.sub(0)], wr=[rvb])
                p.op("v", lambda v, rv=rv: v.tensor_mul(rv, rv, gam.ap),
                     rd=[rvb, gam.sub(0)], wr=[rvb])
                p.op("v", lambda v, rv=rv: v.tensor_add(rv, rv, bet.ap),
                     rd=[rvb, bet.sub(0)], wr=[rvb])
                a2t, bf2 = xtile(c + 1)
                p.dma("y", a2t, out_d[b, c * 128:(c + 1) * 128, :],
                      rd=[out_bufs[b][c]], wr=[bf2])
                p.op("v", lambda v, rv=rv, a2t=a2t: v.tensor_tensor(rv, rv, a2t, ALU.add),
                     rd=[rvb, bf2], wr=[rvb])
                emit_stats(rv, rvb, b * SC + c)
                p.dma("g", out_d[b, c * 128:(c + 1) * 128, :], rv,
                      rd=[rvb], wr=[out_bufs[b][c]])
                bg_pump(2)
            mem.free(aol); mem.free(res); mem.free(wnat)
        for r in (projT, ones128, gam, bet, bias, b2sq, inner, qptmp, den2, aoTp):
            mem.free(r)

    def emit_round_q(src_ap, src_bufs, scale_ap, scale_bufs, dst_ap, dst_bufs,
                     W, rscr, iscr):
        for j0 in range(0, W, 1024):
            jw = min(1024, W - j0)
            yp = rscr.view(2048, 2048 + jw)
            r0f = rscr.view(0, jw)
            cb = rscr.view(1024, 1024 + jw)
            it = iscr.view(0, jw)
            d = dst_ap[:, j0:j0 + jw]
            src_pc = src_ap[:, j0:j0 + jw]
            sb = rscr.sub(0)
            ib = iscr.sub(0)
            s_bufs = src_bufs(j0, jw) if callable(src_bufs) else list(src_bufs)
            d_bufs = dst_bufs(j0, jw) if callable(dst_bufs) else list(dst_bufs)
            p.op("s", lambda s, yp=yp, src_pc=src_pc: s.activation(
                yp, src_pc, AF.Identity, bias=C2565, scale=scale_ap),
                rd=s_bufs + list(scale_bufs) + [consts.sub(0)], wr=[sb])
            p.op("v", lambda v, it=it, yp=yp: v.tensor_copy(it, yp),
                 rd=[sb], wr=[ib])
            p.op("v", lambda v, r0f=r0f, it=it: v.tensor_single_scalar(
                r0f, it, -264.0, ALU.add), rd=[ib], wr=[sb])
            p.op("v", lambda v, yp=yp, r0f=r0f: v.tensor_tensor(
                yp, yp, r0f, ALU.subtract), rd=[sb], wr=[sb])
            for thr, w in ((264.0, 8.0), (260.0, 4.0), (258.0, 2.0), (257.0, 1.0)):
                p.op("v", lambda v, cb=cb, yp=yp, thr=thr: v.tensor_single_scalar(
                    cb, yp, thr, ALU.is_ge), rd=[sb], wr=[sb])
                acc, accb = (r0f, [sb]) if w == 8.0 else (d, d_bufs)
                p.op("v", lambda v, d=d, cb=cb, acc=acc, w=w: v.scalar_tensor_tensor(
                    d, cb, w, acc, ALU.mult, ALU.add),
                    rd=[sb] + accb, wr=d_bufs)
                if w != 1.0:
                    p.op("v", lambda v, yp=yp, cb=cb, w=w: v.scalar_tensor_tensor(
                        yp, cb, -w, yp, ALU.mult, ALU.add), rd=[sb], wr=[sb])

    def quant_weight_tern(src_dram, dst_dram, R, C, dbuf, onescol):
        wt = mem.alloc("twt", F32, C)
        qt = mem.alloc("twq", BF16, C)
        acc = mem.alloc("tacc", F32, 4)
        scr = mem.alloc("tscr", F32, 1024)
        out = mem.alloc("swb", F32, 2)
        p.op("v", lambda v: v.memset(acc.ap[:, 0:1], 0.0), wr=[acc.sub(0)])
        for r0 in range(0, R, 128):
            p.dma("y", wt.ap, src_dram[r0:r0 + 128, :], wr=[wt.sub(0)])
            p.op("v", lambda v: v.tensor_reduce(
                acc.ap[:, 1:2], wt.ap, AX.X, ALU.add, apply_absolute_value=True),
                rd=[wt.sub(0)], wr=[acc.sub(0)])
            p.op("v", lambda v: v.tensor_add(acc.ap[:, 0:1], acc.ap[:, 0:1], acc.ap[:, 1:2]),
                 rd=[acc.sub(0)], wr=[acc.sub(0)])
        p.op("t", lambda t: t.matmul(pb[5][0:1, 0:1], onescol.ap, acc.ap[:, 0:1],
                                     start=True, stop=True, skip_group_check=True),
             rd=[onescol.sub(0), acc.sub(0)], wr=[pb_buf[5]])
        p.op("v", lambda v, n=R * C: v.tensor_scalar(
            acc.ap[0:1, 2:3], pb[5][0:1, 0:1], 1.0 / n, 1e-5, ALU.mult, ALU.max),
            rd=[pb_buf[5]], wr=[acc.sub(0)])
        p.op("v", lambda v: v.reciprocal(acc.ap[0:1, 3:4], acc.ap[0:1, 2:3]),
             rd=[acc.sub(0)], wr=[acc.sub(0)])
        p.op("t", lambda t: t.matmul(pb[5][0:128, 0:2], conesf.ap[0:1, 0:128],
                                     acc.ap[0:1, 2:4], start=True, stop=True),
             rd=[conesf.sub(0), acc.sub(0)], wr=[pb_buf[5]])
        p.op("v", lambda v: v.tensor_copy(out.ap, pb[5][0:128, 0:2]),
             rd=[pb_buf[5]], wr=[out.sub(0)])
        for r0 in range(0, R, 128):
            p.dma("y", wt.ap, src_dram[r0:r0 + 128, :], wr=[wt.sub(0)])
            p.op("v", lambda v: v.tensor_scalar_mul(
                wt.ap, wt.ap, out.ap[:, 1:2]),
                rd=[wt.sub(0), out.sub(0)], wr=[wt.sub(0)])
            for j0 in range(0, C, 1024):
                jw = min(1024, C - j0)
                p.op("v", lambda v, j0=j0, jw=jw: v.tensor_single_scalar(
                    scr.ap[:, 0:jw], wt.ap[:, j0:j0 + jw], -0.5, ALU.is_lt),
                    rd=[wt.sub(0)], wr=[scr.sub(0)])
                p.op("v", lambda v, j0=j0, jw=jw: v.scalar_tensor_tensor(
                    qt.ap[:, j0:j0 + jw], wt.ap[:, j0:j0 + jw], 0.5,
                    scr.ap[:, 0:jw], ALU.is_gt, ALU.subtract),
                    rd=[wt.sub(0), scr.sub(0)], wr=[qt.sub(0)])
            p.dma("g", dst_dram[r0:r0 + 128, :], qt.ap, rd=[qt.sub(0)], wr=[dbuf])
        for r in (wt, qt, acc, scr):
            mem.free(r)
        return out

    def emit_block3():
        bg_flush()
        mem.free(wtbg); mem.free(scrbg)
        sw1, sw2 = swb1, swb2
        finalize_stats(EPS5)
        st = stats
        rr = mem.alloc("rr", F32, NSC)
        p.op("v", lambda v: v.tensor_mul(rr.ap, st["rstd"].ap, st["rstd"].ap),
             rd=[st["rstd"].sub(0)], wr=[rr.sub(0)])
        p.op("v", lambda v: v.tensor_mul(rr.ap, rr.ap, st["var"].ap),
             rd=[rr.sub(0), st["var"].sub(0)], wr=[rr.sub(0)])
        p.op("s", lambda s: s.activation(rr.ap, rr.ap, AF.Sqrt, bias=EPS6),
             rd=[rr.sub(0), consts.sub(0)], wr=[rr.sub(0)])
        p.op("v", lambda v: v.reciprocal(rr.ap, rr.ap), rd=[rr.sub(0)], wr=[rr.sub(0)])
        p.op("v", lambda v: v.tensor_mul(rr.ap, rr.ap, st["rstd"].ap),
             rd=[rr.sub(0), st["rstd"].sub(0)], wr=[rr.sub(0)])
        BCW = min(2048, H2)
        PB1 = H2 // BCW
        assert PB1 <= 2 and D <= BCW
        NP1 = 32 * (PB1 - 1) + 1
        h2tiles = [(n0, min(512, H2 - n0)) for n0 in range(0, H2, 512)]
        dtiles = [(n0, min(512, D - n0)) for n0 in range(0, D, 512)]
        nparts = H2 // D
        NH5 = H2 // 512
        wq1T = mem.alloc("wq1T", BF16, KD * H2, nsub=KD * NH5)
        wq2T = mem.alloc("wq2T", BF16, H2C * D)
        for nt in range(NH5):
            for k in range(KD):
                for i in range(nt * 4, min(nt * 4 + 4, H2C)):
                    dmat(wq1T.ap[:, k * H2 + i * 128: k * H2 + (i + 1) * 128],
                         wq1_d[i * 128:(i + 1) * 128, k * 128:(k + 1) * 128],
                         rd=[wqd_bufs["1"]], wr=[wq1T.sub(k * NH5 + nt)])
        for i in range(H2C):
            for k in range(KD):
                dmat(wq2T.ap[:, i * D + k * 128: i * D + (k + 1) * 128],
                     wq2_d[k * 128:(k + 1) * 128, i * 128:(i + 1) * 128],
                     rd=[wqd_bufs["2"]], wr=[wq2T.sub(0)], q="s")
        browT = mem.alloc("browT", BF16, BCW + 384)
        for pp in range(PB1):
            p.dma("g", browT.ap[32 * pp:32 * pp + 1, 0:BCW],
                  b31[pp * BCW:(pp + 1) * BCW].unsqueeze(0), wr=[browT.sub(0)])
        p.dma("g", browT.ap[64:65, 0:D], b32.unsqueeze(0), wr=[browT.sub(0)])
        ivrow = mem.alloc("ivrow", BF16, 128)
        ivb = mem.alloc("ivb", BF16, 4)
        ivf = mem.alloc("ivf", F32, 4)
        xq1 = mem.alloc("xq1", BF16, 2 * D, nsub=2)
        xq1T = mem.alloc("xq1T", BF16, KD * 128)
        hth = mem.alloc("hth", BF16, H2, nsub=H2 // 512)
        xq2T = mem.alloc("xq2T", BF16, H2C * 128)
        xnf = mem.alloc("xnf", F32, D)
        res = mem.alloc("res3", F32, 2 * D, nsub=2)
        rowm = mem.alloc("rowm", F32, 16, nsub=2)
        modscr = mem.alloc("modscr", F32, 3 * 1024)
        iscr = mem.alloc("iscr", I32, 1024)
        msc1 = mem.alloc("msc1", F32, NSC)
        msc2 = mem.alloc("msc2", F32, NSC)

        def iv_to_row(msc_col, msc_bufs, sl, which, g):
            iv_c = ivf.ap[:, which * 2 + (g % 2): which * 2 + (g % 2) + 1]
            ivb_c = ivb.ap[:, which * 2 + (g % 2): which * 2 + (g % 2) + 1]
            p.op("v", lambda v: v.reciprocal(iv_c, msc_col),
                 rd=list(msc_bufs), wr=[ivf.sub(0)])
            p.op("s", lambda s: s.copy(ivb_c, iv_c), rd=[ivf.sub(0)], wr=[ivb.sub(0)])
            ti = tp_ctr[0] & 1
            tp_ctr[0] += 1
            p.op("t", lambda t, ti=ti: t.transpose(tp[ti][0:1, 0:128], ivb_c, cident.ap),
                 rd=[ivb.sub(0), cident.sub(0)], wr=[tp_buf[ti]])
            rvw = ivrow.ap[0:1, 0:128]
            p.op("s", lambda s, ti=ti: s.copy(rvw, tp[ti][0:1, 0:128]),
                 rd=[tp_buf[ti]], wr=[ivrow.sub(0)])
            pbase, pn = (0, NP1) if which == 0 else (64, 1)
            p.op("t", lambda t: t.matmul(pb[4][pbase:pbase + pn, 0:128],
                                         cmask.ap[0:1, 0:pn], rvw,
                                         start=True, stop=True, skip_group_check=True),
                 rd=[ivrow.sub(0), cmask.sub(0)], wr=[pb_buf[4]])
            dstc = BCW if which == 0 else BCW + 128 + sl * 128
            p.op("s", lambda s: s.copy(browT.ap[pbase:pbase + pn, dstc:dstc + 128],
                                       pb[4][pbase:pbase + pn, 0:128]),
                 rd=[pb_buf[4]], wr=[browT.sub(0)])

        def emit_pets2(g):
            for i in range(H2C):
                pet(xq2T.ap[:, i * 128:(i + 1) * 128],
                    hth.ap[:, i * 128:(i + 1) * 128],
                    rd=[hth.sub(i // 4)], wr=[xq2T.sub(0)])

        def emit_mm2(g):
            b, c = g // SC, g % SC
            sl2 = g % 2
            rv = res.view(sl2 * D, sl2 * D + D)
            rvb = res.sub(sl2)
            a, bf = xtile(g)
            for dn, (n0, nw) in enumerate(dtiles):
                bank = 2 + dn % 2
                p2i = 64
                bc2 = BCW + 128 + sl2 * 128
                for i in range(H2C):
                    p.op("t", lambda t, bank=bank, i=i, n0=n0, nw=nw: t.matmul(
                        pb[bank][:, 0:nw], xq2T.ap[:, i * 128:(i + 1) * 128],
                        wq2T.ap[:, i * D + n0: i * D + n0 + nw],
                        start=(i == 0), stop=(i == H2C - 1)),
                        rd=[xq2T.sub(0), wq2T.sub(0)], wr=[pb_buf[bank]])
                    if i == 0:
                        p.op("t", lambda t, bank=bank, n0=n0, nw=nw, p2i=p2i, bc2=bc2: t.matmul(
                            pb[bank][:, 0:nw], browT.ap[p2i:p2i + 1, bc2:bc2 + 128],
                            browT.ap[p2i:p2i + 1, n0 % BCW: n0 % BCW + nw],
                            start=False, stop=False, skip_group_check=True),
                            rd=[browT.sub(0)], wr=[pb_buf[bank]])
                p.op("s", lambda s, bank=bank, n0=n0, nw=nw, rv=rv, g=g: s.activation(
                    rv[:, n0:n0 + nw], pb[bank][:, 0:nw], AF.Copy, bias=0.0,
                    scale=msc2.ap[:, g:g + 1]),
                    rd=[pb_buf[bank], msc2.sub(0)], wr=[rvb])
            p.op("v", lambda v, rv=rv, a=a: v.tensor_tensor(rv, rv, a, ALU.add),
                 rd=[rvb, bf], wr=[rvb])
            p.dma("g", out_d[b, c * 128:(c + 1) * 128, :], rv,
                  rd=[rvb], wr=[out_bufs[b][c]])

        for g in range(NB * SC):
            b, c = g // SC, g % SC
            sl1 = g % 2
            ro = sl1 * 8
            a, bf = xtile(g)
            p.dma("s", a, out_d[b, c * 128:(c + 1) * 128, :],
                  rd=[out_bufs[b][c]], wr=[bf])
            p.op("v", lambda v, a=a, g=g: v.tensor_scalar(
                xnf.ap, a, st["mu"].ap[:, g:g + 1], rr.ap[:, g:g + 1],
                ALU.subtract, ALU.mult),
                rd=[bf, st["mu"].sub(0), rr.sub(0)], wr=[xnf.sub(0)])
            p.op("v", lambda v, ro=ro: v.tensor_reduce(
                rowm.ap[:, ro:ro + 1], xnf.ap, AX.X, ALU.max, apply_absolute_value=True),
                rd=[xnf.sub(0)], wr=[rowm.sub(sl1)])
            p.op("v", lambda v, ro=ro: v.tensor_scalar_max(
                rowm.ap[:, ro:ro + 1], rowm.ap[:, ro:ro + 1], 1e-5),
                rd=[rowm.sub(sl1)], wr=[rowm.sub(sl1)])
            p.op("v", lambda v, ro=ro: v.reciprocal(
                rowm.ap[:, ro + 1:ro + 2], rowm.ap[:, ro:ro + 1]),
                rd=[rowm.sub(sl1)], wr=[rowm.sub(sl1)])
            p.op("v", lambda v, ro=ro: v.tensor_scalar_mul(
                rowm.ap[:, ro + 1:ro + 2], rowm.ap[:, ro + 1:ro + 2], 127.0),
                rd=[rowm.sub(sl1)], wr=[rowm.sub(sl1)])
            emit_round_q(xnf.ap, [xnf.sub(0)], rowm.ap[:, ro + 1:ro + 2],
                         [rowm.sub(sl1)], xq1.ap[:, sl1 * D:(sl1 + 1) * D],
                         [xq1.sub(sl1)], D, modscr, iscr)
            p.op("v", lambda v, g=g, ro=ro: v.tensor_scalar_mul(
                msc1.ap[:, g:g + 1], rowm.ap[:, ro:ro + 1], 1.0 / 127.0),
                rd=[rowm.sub(sl1)], wr=[msc1.sub(0)])
            p.op("v", lambda v, g=g: v.tensor_mul(
                msc1.ap[:, g:g + 1], msc1.ap[:, g:g + 1], sw1.ap[:, 0:1]),
                rd=[msc1.sub(0), sw1.sub(0)], wr=[msc1.sub(0)])
            iv_to_row(msc1.ap[:, g:g + 1], [msc1.sub(0)], sl1, 0, g)
            if g > 0:
                emit_pets2(g - 1)
            for k in range(KD):
                pet(xq1T.ap[:, k * 128:(k + 1) * 128],
                    xq1.ap[:, sl1 * D + k * 128: sl1 * D + (k + 1) * 128],
                    rd=[xq1.sub(sl1)], wr=[xq1T.sub(0)])
            for nt, (n0, nw) in enumerate(h2tiles):
                bank = nt % 2
                p1i = 32 * (n0 // BCW)
                bc1 = BCW
                for k in range(KD):
                    p.op("t", lambda t, bank=bank, k=k, n0=n0, nw=nw: t.matmul(
                        pb[bank][:, 0:nw], xq1T.ap[:, k * 128:(k + 1) * 128],
                        wq1T.ap[:, k * H2 + n0: k * H2 + n0 + nw],
                        start=(k == 0), stop=(k == KD - 1)),
                        rd=[xq1T.sub(0), wq1T.sub(k * NH5 + nt)], wr=[pb_buf[bank]])
                    if k == 0:
                        p.op("t", lambda t, bank=bank, n0=n0, nw=nw, p1i=p1i, bc1=bc1: t.matmul(
                            pb[bank][:, 0:nw], browT.ap[p1i:p1i + 1, bc1:bc1 + 128],
                            browT.ap[p1i:p1i + 1, n0 % BCW: n0 % BCW + nw],
                            start=False, stop=False, skip_group_check=True),
                            rd=[browT.sub(0)], wr=[pb_buf[bank]])
                p.op("s", lambda s, bank=bank, n0=n0, nw=nw, g=g: s.activation(
                    hth.ap[:, n0:n0 + nw], pb[bank][:, 0:nw], AF.Gelu, bias=0.0,
                    scale=msc1.ap[:, g:g + 1]),
                    rd=[pb_buf[bank], msc1.sub(0)], wr=[hth.sub(nt)])
                for j in range((nt * 512) // D, ((nt + 1) * 512) // D):
                    jb = [hth.sub(s) for s in
                          range((j * D) // 512, ((j + 1) * D - 1) // 512 + 1)]
                    p.op("s", lambda s, j=j, ro=ro: s.activation(
                        sq_scratch.ap[:, 0:D], hth.ap[:, j * D:(j + 1) * D],
                        AF.Square, accum_out=rowm.ap[:, ro + 2 + j: ro + 3 + j]),
                        rd=jb, wr=[sq_scratch.sub(0), rowm.sub(sl1)])
            for j in range(1, nparts):
                p.op("v", lambda v, j=j, ro=ro: v.tensor_add(
                    rowm.ap[:, ro + 2:ro + 3], rowm.ap[:, ro + 2:ro + 3],
                    rowm.ap[:, ro + 2 + j:ro + 3 + j]),
                    rd=[rowm.sub(sl1)], wr=[rowm.sub(sl1)])
            p.op("v", lambda v, ro=ro: v.tensor_scalar_mul(
                rowm.ap[:, ro + 2:ro + 3], rowm.ap[:, ro + 2:ro + 3], 1.0 / H2),
                rd=[rowm.sub(sl1)], wr=[rowm.sub(sl1)])
            p.op("s", lambda s, ro=ro: s.activation(
                rowm.ap[:, ro + 3:ro + 4], rowm.ap[:, ro + 2:ro + 3], AF.Sqrt, bias=EPS6),
                rd=[rowm.sub(sl1), consts.sub(0)], wr=[rowm.sub(sl1)])
            p.op("v", lambda v, ro=ro: v.reciprocal(
                rowm.ap[:, ro + 3:ro + 4], rowm.ap[:, ro + 3:ro + 4]),
                rd=[rowm.sub(sl1)], wr=[rowm.sub(sl1)])
            p.op("v", lambda v, ro=ro: v.tensor_reduce(
                rowm.ap[:, ro + 4:ro + 5], hth.ap, AX.X, ALU.max,
                apply_absolute_value=True),
                rd=list(hth.all_bufs()), wr=[rowm.sub(sl1)])
            p.op("v", lambda v, ro=ro: v.tensor_mul(
                rowm.ap[:, ro + 4:ro + 5], rowm.ap[:, ro + 4:ro + 5],
                rowm.ap[:, ro + 3:ro + 4]),
                rd=[rowm.sub(sl1)], wr=[rowm.sub(sl1)])
            p.op("v", lambda v, ro=ro: v.tensor_scalar_max(
                rowm.ap[:, ro + 4:ro + 5], rowm.ap[:, ro + 4:ro + 5], 1e-5),
                rd=[rowm.sub(sl1)], wr=[rowm.sub(sl1)])
            p.op("v", lambda v, ro=ro: v.reciprocal(
                rowm.ap[:, ro + 5:ro + 6], rowm.ap[:, ro + 4:ro + 5]),
                rd=[rowm.sub(sl1)], wr=[rowm.sub(sl1)])
            p.op("v", lambda v, ro=ro: v.tensor_scalar_mul(
                rowm.ap[:, ro + 5:ro + 6], rowm.ap[:, ro + 5:ro + 6], 127.0),
                rd=[rowm.sub(sl1)], wr=[rowm.sub(sl1)])
            p.op("v", lambda v, ro=ro: v.tensor_mul(
                rowm.ap[:, ro + 5:ro + 6], rowm.ap[:, ro + 5:ro + 6],
                rowm.ap[:, ro + 3:ro + 4]),
                rd=[rowm.sub(sl1)], wr=[rowm.sub(sl1)])
            p.op("v", lambda v, g=g, ro=ro: v.tensor_scalar_mul(
                msc2.ap[:, g:g + 1], rowm.ap[:, ro + 4:ro + 5], 1.0 / 127.0),
                rd=[rowm.sub(sl1)], wr=[msc2.sub(0)])
            p.op("v", lambda v, g=g: v.tensor_mul(
                msc2.ap[:, g:g + 1], msc2.ap[:, g:g + 1], sw2.ap[:, 0:1]),
                rd=[msc2.sub(0), sw2.sub(0)], wr=[msc2.sub(0)])
            iv_to_row(msc2.ap[:, g:g + 1], [msc2.sub(0)], sl1, 1, g)
            hsubs = lambda j0, jw: [hth.sub(s) for s in
                                    range(j0 // 512, (j0 + jw - 1) // 512 + 1)]
            emit_round_q(hth.ap, hsubs, rowm.ap[:, ro + 5:ro + 6],
                         [rowm.sub(sl1)], hth.ap, hsubs, H2, modscr, iscr)
            if g > 0:
                emit_mm2(g - 1)
        emit_pets2(NB * SC - 1)
        emit_mm2(NB * SC - 1)
        for r in (wq1T, wq2T, browT, ivrow, ivb, ivf, xq1, xq1T, hth, xq2T,
                  xnf, res, rowm, modscr, iscr, msc1, msc2, rr):
            mem.free(r)

    nb_emit = cfg.get("nblocks", 4)
    emit_block0()
    if nb_emit >= 2:
        emit_block1()
    if nb_emit >= 3:
        emit_block2()
    if nb_emit >= 4:
        emit_block3()
    p.finish()
    p.build()
    stack.pop_all()
    return


def make_inputs_per_core(inputs, cfg, core):
    NB = cfg["NB"]
    d = {}
    x = inputs["L0_attn_x"]
    d["L0_attn_x"] = np.ascontiguousarray(x[core * NB:(core + 1) * NB])
    for k, v in inputs.items():
        if k != "L0_attn_x":
            d[k] = np.ascontiguousarray(v)
    m = np.zeros((128, 128), dtype=ml_dtypes.bfloat16)
    iu = np.triu_indices(128)
    mm = np.zeros((128, 128), dtype=np.float32)
    mm[iu] = 1.0
    d["cmask"] = mm.astype(ml_dtypes.bfloat16)
    d["cident"] = np.eye(128, dtype=np.float32).astype(ml_dtypes.bfloat16)
    d["cones"] = np.ones((1, 128), dtype=np.float32)
    return d


def kernel(**inputs):
    cfg = REAL
    nc = bass.Bass("TRN2", target_bir_lowering=False, debug=False)
    build_program(nc, cfg)
    in_maps = [make_inputs_per_core(inputs, cfg, c) for c in range(8)]
    res = run_bass_kernel_spmd(nc, in_maps, core_ids=list(range(8)))
    outs = [res.results[c]["out"] for c in range(8)]
    return np.concatenate(outs, axis=0).astype(np.float32)


# version 21 (final)
# speedup vs baseline: 1.1875x; 1.1875x over previous
import contextlib
import numpy as np
import ml_dtypes
import concourse.bass as bass
import concourse.mybir as mybir
from concourse.bass_utils import run_bass_kernel_spmd

F32 = mybir.dt.float32
BF16 = mybir.dt.bfloat16
I32 = mybir.dt.int32
I8 = mybir.dt.int8
AF = mybir.ActivationFunctionType
ALU = mybir.AluOpType
AX = mybir.AxisListType

REAL = dict(NB=2, S=2048, D=1024, MFEAT=256, IFFN=2816, H2=4096)


class Buf:
    __slots__ = ("w", "r")
    def __init__(self):
        self.w = None
        self.r = []

class DmaSem:
    __slots__ = ("key", "handle", "uses")
    def __init__(self, key, handle):
        self.key, self.handle, self.uses = key, handle, 0

ENGS = ("t", "v", "s", "y", "g")

class Prog:
    def __init__(self, nc, stack, n_dma=(10, 6, 6)):
        self.nc = nc
        self.body = {e: [] for e in ENGS}
        self.sem = {}
        for e in ("t", "v", "s", "g"):
            self.sem[e] = stack.enter_context(nc.semaphore(f"se_{e}"))
        self.cnt = {e: 0 for e in ("t", "v", "s", "g")}
        self.pools = {}
        for q, n in zip(("y", "s", "g"), n_dma):
            ps = []
            for i in range(n):
                h = stack.enter_context(nc.semaphore(f"dq_{q}{i}"))
                ps.append(DmaSem(f"dq_{q}{i}", h))
                self.sem[f"dq_{q}{i}"] = h
            self.pools[q] = (ps, [0])
        self.waited = {e: {} for e in ENGS}
        self.final_dma_tokens = {}

    def _waits(self, e, need):
        out = []
        for sk, v in need.items():
            if self.waited[e].get(sk, 0) >= v:
                continue
            self.waited[e][sk] = v
            out.append((self.sem[sk], v))
        return out

    def _need(self, rd, wr):
        need = {}
        def add(tok):
            if tok is None:
                return
            sk, v = tok
            if need.get(sk, 0) < v:
                need[sk] = v
        for b in rd:
            add(b.w)
        for b in wr:
            add(b.w)
            for t_ in b.r:
                add(t_)
        return need

    def op(self, e, emit, rd=(), wr=()):
        need = self._need(rd, wr)
        waits = self._waits(e, need)
        self.cnt[e] += 1
        tok = (e, self.cnt[e])
        sem = self.sem[e]
        val = self.cnt[e]
        def run(eng, waits=waits, emit=emit, sem=sem):
            for h, v in waits:
                eng.wait_ge(h, v)
            emit(eng).then_inc(sem, 1)
        self.body[e].append(run)
        for b in wr:
            b.w = tok
            b.r = []
        for b in rd:
            b.r.append(tok)
        return tok

    def dma(self, q, out_ap, in_ap, rd=(), wr=(), transpose=False):
        pool, idx = self.pools[q]
        s = pool[idx[0] % len(pool)]
        idx[0] += 1
        need = self._need(rd, wr)
        if need.get(s.key, 0) < 16 * s.uses:
            need[s.key] = 16 * s.uses
        waits = self._waits(q if q != "s" else "s", need)
        s.uses += 1
        tok = (s.key, 16 * s.uses)
        def run(eng, waits=waits, o=out_ap, i=in_ap, h=s.handle, tr=transpose):
            for hh, v in waits:
                eng.wait_ge(hh, v)
            if tr:
                eng.dma_start_transpose(o, i).then_inc(h, 16)
            else:
                eng.dma_start(o, i).then_inc(h, 16)
        self.body[{"y": "y", "s": "s", "g": "g"}[q]].append(run)
        for b in wr:
            b.w = tok
            b.r = []
        for b in rd:
            b.r.append(tok)
        self.final_dma_tokens[s.key] = 16 * s.uses
        return tok

    def finish(self):
        waits = []
        for sk, v in self.final_dma_tokens.items():
            waits.append((self.sem[sk], v))
        for e in ("t", "v", "s", "g"):
            if self.cnt[e]:
                waits.append((self.sem[e], self.cnt[e]))
        def run(eng, waits=waits):
            for h, v in waits:
                eng.wait_ge(h, v)
        self.body["y"].append(run)

    def build(self):
        nc = self.nc
        with nc.Block() as blk:
            @blk.tensor
            def _(t):
                for f in self.body["t"]:
                    f(t)
            @blk.vector
            def _(v):
                for f in self.body["v"]:
                    f(v)
            @blk.scalar
            def _(s):
                for f in self.body["s"]:
                    f(s)
            @blk.sync
            def _(y):
                for f in self.body["y"]:
                    f(y)
            @blk.gpsimd
            def _(g):
                for f in self.body["g"]:
                    f(g)

class Region:
    __slots__ = ("mem", "name", "off", "ncols_pad", "dtype", "cols", "ap", "bufs", "nsub")
    def sub(self, i):
        return self.bufs[i]
    def all_bufs(self):
        return self.bufs
    def view(self, c0, c1):
        return self.mem.arena[self.dtype][:, self.off + c0: self.off + c1]

class Mem:
    def __init__(self, nc, sizes):
        self.arena = {}
        self.free_list = {}
        for dt_, cols in sizes.items():
            nm = {F32: "f32", BF16: "bf16", I32: "i32"}[dt_]
            self.arena[dt_] = nc.alloc_sbuf_tensor(f"arena_{nm}", [128, cols], dt_).ap()
            self.free_list[dt_] = [(0, cols, [])]

    def alloc(self, name, dtype, cols, nsub=1):
        ncols = ((cols + 31) // 32) * 32
        fl = self.free_list[dtype]
        small = ncols <= 192
        idxs = range(len(fl) - 1, -1, -1) if small else range(len(fl))
        for i in idxs:
            off, size, toks = fl[i]
            if size >= ncols:
                break
        else:
            used = sum(e[1] for e in fl)
            raise MemoryError(f"SBUF arena({dtype}) OOM for {name}: need {ncols}, free {used}")
        fl.pop(i)
        if small:
            if size > ncols:
                fl.insert(i, (off, size - ncols, list(toks)))
            off = off + size - ncols
        elif size > ncols:
            fl.insert(i, (off + ncols, size - ncols, list(toks)))
        r = Region()
        r.mem, r.name, r.off, r.ncols_pad, r.dtype, r.cols, r.nsub = \
            self, name, off, ncols, dtype, cols, nsub
        r.ap = self.arena[dtype][:, off: off + cols]
        r.bufs = []
        for j in range(nsub):
            b = Buf()
            b.r = list(toks)
            r.bufs.append(b)
        return r

    def free(self, r):
        toks = []
        for b in r.bufs:
            if b.w is not None:
                toks.append(b.w)
            toks.extend(b.r)
        fl = self.free_list[r.dtype]
        fl.append((r.off, r.ncols_pad, toks))
        fl.sort()
        merged = []
        for off, size, tk in fl:
            if merged and merged[-1][0] + merged[-1][1] == off:
                o0, s0, t0 = merged.pop()
                merged.append((o0, s0 + size, t0 + tk))
            else:
                merged.append((off, size, list(tk)))
        self.free_list[r.dtype] = merged


def build_program(nc, cfg):
    NB, S, D = cfg["NB"], cfg["S"], cfg["D"]
    MFEAT, IFFN, H2 = cfg["MFEAT"], cfg["IFFN"], cfg["H2"]
    SC, KD = S // 128, D // 128
    HEADS, HD = D // 64, 64
    MC = MFEAT // 128
    IC, H2C = IFFN // 128, H2 // 128
    H65 = HEADS * 65
    NT_D = D // 512 if D >= 512 else 1
    NTW = min(512, D)

    stack = contextlib.ExitStack()
    p = Prog(nc, stack)
    mem = Mem(nc, {BF16: 79712, F32: 12320, I32: 1024})

    def din(name, shape):
        t = nc.dram_tensor(name, list(shape), F32, kind="ExternalInput")
        return t.ap()
    x_in = din("L0_attn_x", [NB, S, D])
    w0 = {k: din("L0_attn_" + k, [D, D]) for k in ("Wq", "Wk", "Wv", "Wo")}
    b0_ = {k: din("L0_attn_" + k, [D]) for k in ("bq", "bk", "bv", "bo")}
    wg = din("L0_ffn_w_gate", [IFFN, D]); wu = din("L0_ffn_w_up", [IFFN, D])
    wd = din("L0_ffn_w_down", [D, IFFN])
    w2a = {k: din("L1_attn_" + k, [D, D]) for k in ("q_w", "k_w", "v_w", "out_w")}
    b2a = {k: din("L1_attn_" + k, [D]) for k in ("q_b", "k_b", "v_b", "out_b")}
    proj_in = din("L1_attn_proj", [MFEAT, HD])
    gamma_in = din("L1_attn_gamma", [D]); beta_in = din("L1_attn_beta", [D])
    w31 = din("L1_ffn_w1", [H2, D]); b31 = din("L1_ffn_b1", [H2])
    w32 = din("L1_ffn_w2", [D, H2]); b32 = din("L1_ffn_b2", [D])
    cmask_in = nc.dram_tensor("cmask", [128, 128], BF16, kind="ExternalInput").ap()
    cident_in = nc.dram_tensor("cident", [128, 128], BF16, kind="ExternalInput").ap()
    ones_in = nc.dram_tensor("cones", [1, 128], F32, kind="ExternalInput").ap()

    out_d = nc.dram_tensor("out", [NB, S, D], F32, kind="ExternalOutput").ap()
    xln_spill = nc.dram_tensor("xln_spill", [NB, S, D], F32, kind="Internal").ap()
    h_spill = nc.dram_tensor("h_spill", [NB, S, H2], F32, kind="Internal").ap()
    att_spill = nc.dram_tensor("att_spill", [NB, D, S], BF16, kind="Internal").ap()
    wq_gate_d = nc.dram_tensor("wq_gate", [IFFN, D], BF16, kind="Internal").ap()
    wq_up_d = nc.dram_tensor("wq_up", [IFFN, D], BF16, kind="Internal").ap()
    wq_down_d = nc.dram_tensor("wq_down", [D, IFFN], BF16, kind="Internal").ap()
    wq1_d = nc.dram_tensor("wq1", [H2, D], BF16, kind="Internal").ap()
    wq2_d = nc.dram_tensor("wq2", [D, H2], BF16, kind="Internal").ap()

    out_bufs = [[Buf() for _ in range(SC)] for _ in range(NB)]
    xsp_bufs = [[Buf() for _ in range(SC)] for _ in range(NB)]
    hsp_bufs = [[Buf() for _ in range(SC)] for _ in range(NB)]
    asp_bufs = [[Buf() for _ in range(HEADS)] for _ in range(NB)]
    wqd_bufs = {k: Buf() for k in ("g", "u", "d", "1", "2")}

    pb = []
    pb_buf = []
    for i in range(6):
        t = stack.enter_context(nc.psum_tensor(f"bank{i}", [128, 512], F32))
        pb.append(t.ap())
        pb_buf.append(Buf())
    tp = []
    tp_buf = []
    for i in range(2):
        t = stack.enter_context(nc.psum_tensor(f"tbank{i}", [128, 1024], BF16))
        tp.append(t.ap())
        tp_buf.append(Buf())
    tp_ctr = [0]

    cmask = mem.alloc("cmask", BF16, 128)
    cident = mem.alloc("cident", BF16, 128)
    conesf = mem.alloc("ones", F32, 128)
    p.dma("y", cmask.ap, cmask_in, wr=[cmask.sub(0)])
    p.dma("y", cident.ap, cident_in, wr=[cident.sub(0)])
    p.dma("y", conesf.ap[0:1, :], ones_in, wr=[conesf.sub(0)])

    consts = mem.alloc("consts", F32, 4)
    p.op("v", lambda v: v.memset(consts.ap[:, 0:1], 1e-5), wr=[consts.sub(0)])
    p.op("v", lambda v: v.memset(consts.ap[:, 1:2], 1e-6), wr=[consts.sub(0)])
    p.op("v", lambda v: v.memset(consts.ap[:, 2:3], 256.5), wr=[consts.sub(0)])
    EPS5 = consts.ap[:, 0:1]
    EPS6 = consts.ap[:, 1:2]
    C2565 = consts.ap[:, 2:3]

    NSC = NB * SC
    stats = {}
    for nm in ("musum", "sqsum", "mu", "rstd", "var"):
        stats[nm] = mem.alloc(nm, F32, NSC, nsub=1)
    sq_scratch = mem.alloc("sq_scratch", F32, D, nsub=1)

    def emit_stats(tile_ap, tile_buf, col):
        p.op("v", lambda v, a=tile_ap, c=col: v.reduce_sum(
            stats["musum"].ap[:, c:c + 1], a, AX.X),
            rd=[tile_buf], wr=[stats["musum"].sub(0)])
        p.op("s", lambda s, a=tile_ap, c=col: s.activation(
            sq_scratch.ap[:, 0:a.shape[1]], a, AF.Square,
            accum_out=stats["sqsum"].ap[:, c:c + 1]),
            rd=[tile_buf], wr=[stats["sqsum"].sub(0), sq_scratch.sub(0)])

    def finalize_stats(eps):
        st = stats
        p.op("v", lambda v: v.tensor_scalar_mul(st["mu"].ap, st["musum"].ap, 1.0 / D),
             rd=[st["musum"].sub(0)], wr=[st["mu"].sub(0)])
        p.op("v", lambda v: v.tensor_scalar_mul(st["var"].ap, st["sqsum"].ap, 1.0 / D),
             rd=[st["sqsum"].sub(0)], wr=[st["var"].sub(0)])
        p.op("v", lambda v: v.tensor_mul(st["rstd"].ap, st["mu"].ap, st["mu"].ap),
             rd=[st["mu"].sub(0)], wr=[st["rstd"].sub(0)])
        p.op("v", lambda v: v.tensor_sub(st["var"].ap, st["var"].ap, st["rstd"].ap),
             rd=[st["var"].sub(0), st["rstd"].sub(0)], wr=[st["var"].sub(0)])
        p.op("s", lambda s: s.activation(st["rstd"].ap, st["var"].ap, AF.Sqrt, bias=eps),
             rd=[st["var"].sub(0), consts.sub(0)], wr=[st["rstd"].sub(0)])
        p.op("v", lambda v: v.reciprocal(st["rstd"].ap, st["rstd"].ap),
             rd=[st["rstd"].sub(0)], wr=[st["rstd"].sub(0)])

    def load_bias_tile(dst_region, dram_vec, cols, q="g"):
        p.dma(q, dst_region.ap[:, 0:cols],
              dram_vec.unsqueeze(0).partition_broadcast(128).squeeze(1),
              wr=[dst_region.sub(0)])

    def dmat(dst_ap, src_ap, rd, wr, q="y"):
        p.dma(q, dst_ap, src_ap, rd=rd, wr=wr, transpose=True)

    def pet(dst_ap, src_ap, rd, wr):
        i = tp_ctr[0] & 1
        tp_ctr[0] += 1
        p.op("t", lambda t, src_ap=src_ap, i=i: t.transpose(
            tp[i][:, 0:128], src_ap, cident.ap),
            rd=list(rd) + [cident.sub(0)], wr=[tp_buf[i]])
        p.op("s", lambda s, dst_ap=dst_ap, i=i: s.copy(dst_ap, tp[i][:, 0:128]),
             rd=[tp_buf[i]], wr=list(wr))

    xpool = mem.alloc("xpool", F32, 2 * D, nsub=2)
    def xtile(i):
        j = i % 2
        return xpool.view(j * D, (j + 1) * D), xpool.sub(j)

    bg_queue = []

    def bg_pump(k=1):
        for _ in range(min(k, len(bg_queue))):
            bg_queue.pop(0)()

    def bg_flush():
        while bg_queue:
            bg_queue.pop(0)()

    CWMAX = max(D, IFFN)
    wtbg = mem.alloc("wtbg", F32, CWMAX)
    scrbg = mem.alloc("scrbg", F32, 3 * 512 + 2 * (CWMAX // 16))
    def mk_fp4row(src_dram, dst_dram, r0, C, dbuf):
        def task():
            wv = wtbg.view(0, C)
            p.dma("y", wv, src_dram[r0:r0 + 128, :], wr=[wtbg.sub(0)])
            emit_fp4(wv, [wtbg.sub(0)], wv, [wtbg.sub(0)], C, scrbg, CW=512)
            p.dma("g", dst_dram[r0:r0 + 128, :], wv, rd=[wtbg.sub(0)], wr=[dbuf])
        return task
    for (srcd, dstd, R_, C_, key) in ((wg, wq_gate_d, IFFN, D, "g"),
                                      (wu, wq_up_d, IFFN, D, "u"),
                                      (wd, wq_down_d, D, IFFN, "d")):
        for r0 in range(0, R_, 128):
            bg_queue.append(mk_fp4row(srcd, dstd, r0, C_, wqd_bufs[key]))

    onescol = mem.alloc("onescol", F32, 1)
    p.op("v", lambda v: v.memset(onescol.ap, 1.0), wr=[onescol.sub(0)])
    tacc = mem.alloc("tacc", F32, 4)
    swb1 = mem.alloc("swb1", F32, 2)
    swb2 = mem.alloc("swb2", F32, 2)

    def queue_tern_tasks(src_dram, dst_dram, R, C, dbuf, out):
        TCW = min(1024, CWMAX)
        def t_zero():
            p.op("v", lambda v: v.memset(tacc.ap[:, 0:1], 0.0), wr=[tacc.sub(0)])
        bg_queue.append(t_zero)
        def mk_p1(r0, c0, cw):
            def task():
                wv = wtbg.view(0, cw)
                p.dma("y", wv, src_dram[r0:r0 + 128, c0:c0 + cw], wr=[wtbg.sub(0)])
                p.op("v", lambda v, wv=wv: v.tensor_reduce(
                    tacc.ap[:, 1:2], wv, AX.X, ALU.add, apply_absolute_value=True),
                    rd=[wtbg.sub(0)], wr=[tacc.sub(0)])
                p.op("v", lambda v: v.tensor_add(tacc.ap[:, 0:1], tacc.ap[:, 0:1], tacc.ap[:, 1:2]),
                     rd=[tacc.sub(0)], wr=[tacc.sub(0)])
            return task
        for r0 in range(0, R, 128):
            for c0 in range(0, C, TCW):
                bg_queue.append(mk_p1(r0, c0, min(TCW, C - c0)))
        def t_mid():
            p.op("t", lambda t: t.matmul(pb[0][0:1, 0:1], onescol.ap, tacc.ap[:, 0:1],
                                         start=True, stop=True, skip_group_check=True),
                 rd=[onescol.sub(0), tacc.sub(0)], wr=[pb_buf[0]])
            p.op("v", lambda v, n=R * C: v.tensor_scalar(
                tacc.ap[0:1, 2:3], pb[0][0:1, 0:1], 1.0 / n, 1e-5, ALU.mult, ALU.max),
                rd=[pb_buf[0]], wr=[tacc.sub(0)])
            p.op("v", lambda v: v.reciprocal(tacc.ap[0:1, 3:4], tacc.ap[0:1, 2:3]),
                 rd=[tacc.sub(0)], wr=[tacc.sub(0)])
            p.op("t", lambda t: t.matmul(pb[0][0:128, 0:2], conesf.ap[0:1, 0:128],
                                         tacc.ap[0:1, 2:4], start=True, stop=True,
                                         skip_group_check=True),
                 rd=[conesf.sub(0), tacc.sub(0)], wr=[pb_buf[0]])
            p.op("v", lambda v: v.tensor_copy(out.ap, pb[0][0:128, 0:2]),
                 rd=[pb_buf[0]], wr=[out.sub(0)])
        bg_queue.append(t_mid)
        def mk_p2(r0, c0, cw):
            def task():
                wv = wtbg.view(0, cw)
                sv = scrbg.view(0, cw)
                p.dma("y", wv, src_dram[r0:r0 + 128, c0:c0 + cw], wr=[wtbg.sub(0)])
                p.op("v", lambda v, wv=wv: v.tensor_scalar_mul(wv, wv, out.ap[:, 1:2]),
                     rd=[wtbg.sub(0), out.sub(0)], wr=[wtbg.sub(0)])
                p.op("v", lambda v, wv=wv, sv=sv: v.tensor_single_scalar(
                    sv, wv, -0.5, ALU.is_lt), rd=[wtbg.sub(0)], wr=[scrbg.sub(0)])
                p.op("v", lambda v, wv=wv, sv=sv: v.scalar_tensor_tensor(
                    wv, wv, 0.5, sv, ALU.is_gt, ALU.subtract),
                    rd=[wtbg.sub(0), scrbg.sub(0)], wr=[wtbg.sub(0)])
                p.dma("g", dst_dram[r0:r0 + 128, c0:c0 + cw], wv,
                      rd=[wtbg.sub(0)], wr=[dbuf])
            return task
        for r0 in range(0, R, 128):
            for c0 in range(0, C, TCW):
                bg_queue.append(mk_p2(r0, c0, min(TCW, C - c0)))

    for b in range(NB):
        for c in range(SC):
            a, bf = xtile(b * SC + c)
            p.dma("y", a, x_in[b, c * 128:(c + 1) * 128, :], wr=[bf])
            emit_stats(a, bf, b * SC + c)

    def emit_block0():
        finalize_stats(EPS5)
        st = stats
        bias = mem.alloc("b0bias", F32, D)
        wnat = mem.alloc("b0w", BF16, KD * D)
        expT = mem.alloc("expT", BF16, 4 * 512, nsub=4)
        den = mem.alloc("den", F32, 2 * 512, nsub=2)
        for b in range(NB):
            xlnT = mem.alloc("xlnT", BF16, KD * S, nsub=SC)
            xlnbf = mem.alloc("xlnbf", BF16, 2 * D, nsub=2)
            qT = mem.alloc("qT", BF16, KD * S, nsub=KD * SC)
            kT = mem.alloc("kT", BF16, KD * S, nsub=KD * SC)
            vaug = mem.alloc("vaug", BF16, SC * H65, nsub=SC)
            p.op("v", lambda v, a=vaug.ap: v.memset(
                a.rearrange("p (c h e) -> p (c h) e", e=65, h=HEADS)[:, :, 64:65], 1.0),
                wr=list(vaug.all_bufs()))
            for c in range(SC):
                a, bf = xtile(c)
                p.dma("y", a, x_in[b, c * 128:(c + 1) * 128, :], wr=[bf])
                xv = xlnbf.view((c % 2) * D, (c % 2 + 1) * D)
                xvb = xlnbf.sub(c % 2)
                col = b * SC + c
                p.op("v", lambda v, a=a, xv=xv, col=col: v.tensor_scalar(
                    xv, a, st["mu"].ap[:, col:col + 1], st["rstd"].ap[:, col:col + 1],
                    ALU.subtract, ALU.mult),
                    rd=[bf, st["mu"].sub(0), st["rstd"].sub(0)], wr=[xvb])
                for k in range(KD):
                    pet(xlnT.ap[:, k * S + c * 128: k * S + c * 128 + 128],
                        xv[:, k * 128:(k + 1) * 128], rd=[xvb], wr=[xlnT.sub(c)])
            for ti, (wkey, bkey) in enumerate((("Wq", "bq"), ("Wk", "bk"), ("Wv", "bv"))):
                for k in range(KD):
                    p.dma("g", wnat.ap[:, k * D:(k + 1) * D],
                          w0[wkey][k * 128:(k + 1) * 128, :], wr=[wnat.sub(0)])
                load_bias_tile(bias, b0_[bkey], D)
                if ti == 0:
                    p.op("v", lambda v: v.tensor_scalar_mul(bias.ap[:, 0:D], bias.ap[:, 0:D], 0.125),
                         rd=[bias.sub(0)], wr=[bias.sub(0)])
                qsc = 0.125 if ti == 0 else 1.0
                for c in range(SC):
                    dst = None
                    if ti < 2:
                        dst = xlnbf.view((c % 2) * D, (c % 2 + 1) * D)
                        dstb = xlnbf.sub(c % 2)
                    for nh in range(D // NTW):
                        bank = (c % 2) * 2 + (nh % 2)
                        for k in range(KD):
                            p.op("t", lambda t, bank=bank, k=k, c=c, nh=nh, wk=wkey:
                                 t.matmul(pb[bank][:, 0:NTW],
                                          xlnT.ap[:, k * S + c * 128: k * S + c * 128 + 128],
                                          wnat.ap[:, k * D + nh * NTW: k * D + (nh + 1) * NTW],
                                          start=(k == 0), stop=(k == KD - 1)),
                                 rd=[xlnT.sub(c), wnat.sub(0)], wr=[pb_buf[bank]])
                        if ti < 2:
                            p.op("v", lambda v, bank=bank, nh=nh, dst=dst, qsc=qsc:
                                 v.scalar_tensor_tensor(
                                     dst[:, nh * NTW:(nh + 1) * NTW], pb[bank][:, 0:NTW],
                                     qsc, bias.ap[:, nh * NTW:(nh + 1) * NTW],
                                     ALU.mult, ALU.add),
                                 rd=[pb_buf[bank], bias.sub(0)], wr=[dstb])
                        else:
                            va = vaug.ap.rearrange("p (c h e) -> p c h e", e=65, h=HEADS)
                            hpt = NTW // 64
                            p.op("v", lambda v, bank=bank, nh=nh, c=c, va=va, hpt=hpt:
                                 v.scalar_tensor_tensor(
                                     va[:, c, nh * hpt:(nh + 1) * hpt, 0:64],
                                     pb[bank][:, 0:NTW].rearrange("p (h e) -> p h e", e=64),
                                     1.0,
                                     bias.ap[:, nh * NTW:(nh + 1) * NTW].rearrange(
                                         "p (h e) -> p h e", e=64),
                                     ALU.mult, ALU.add),
                                 rd=[pb_buf[bank], bias.sub(0)], wr=[vaug.sub(c)])
                    if ti < 2:
                        tgt = qT if ti == 0 else kT
                        for k in range(KD):
                            pet(tgt.ap[:, k * S + c * 128: k * S + c * 128 + 128],
                                dst[:, k * 128:(k + 1) * 128],
                                rd=[dstb], wr=[tgt.sub(k * SC + c)])
            mem.free(xlnT)
            aoT = mem.alloc("aoT", BF16, KD * S, nsub=KD * SC)
            NSQ = 512 if S >= 512 else S
            nsq_t = S // NSQ
            skpt = NSQ // 128
            LA = 4
            for h in range(HEADS):
                fc, half = h // 2, h % 2
                rows = slice(half * 64, half * 64 + 64)
                for tq in range(nsq_t):
                    qsub = [qT.sub(fc * SC + i) for i in range(tq * skpt,
                                                              min(SC, (tq + 1) * skpt))]
                    nsk = skpt * (tq + 1)
                    def emit_scores(sk, tq=tq, fc=fc, rows=rows, qsub=qsub):
                        col0 = max(0, sk * 128 - tq * NSQ)
                        bank = 4 + (sk % 2)
                        p.op("t", lambda t, bank=bank, sk=sk, col0=col0, tq=tq, fc=fc, rows=rows:
                             t.matmul(pb[bank][:, col0:NSQ],
                                      kT.ap[rows, fc * S + sk * 128: fc * S + sk * 128 + 128],
                                      qT.ap[rows, fc * S + tq * NSQ + col0: fc * S + (tq + 1) * NSQ],
                                      start=True, stop=True),
                             rd=[kT.sub(fc * SC + sk)] + qsub, wr=[pb_buf[bank]])
                        ev = expT.view((sk % LA) * 512, (sk % LA) * 512 + NSQ)
                        evb = expT.sub(sk % LA)
                        p.op("s", lambda s, bank=bank, col0=col0, ev=ev:
                             s.activation(ev[:, col0:NSQ], pb[bank][:, col0:NSQ], AF.Exp),
                             rd=[pb_buf[bank]], wr=[evb])
                        if sk >= skpt * tq:
                            dcol = sk * 128 - tq * NSQ
                            p.op("v", lambda v, ev=ev, dcol=dcol:
                                 v.tensor_tensor(ev[:, dcol:dcol + 128], ev[:, dcol:dcol + 128],
                                                 cmask.ap, ALU.mult),
                                 rd=[evb, cmask.sub(0)], wr=[evb])
                    par = (h * nsq_t + tq) % 2
                    nbk, bbk = (0, 1) if par == 0 else (2, 3)
                    def emit_num(sk, tq=tq, h=h, nsk=nsk, nbk=nbk):
                        col0 = max(0, sk * 128 - tq * NSQ)
                        ev = expT.view((sk % LA) * 512, (sk % LA) * 512 + NSQ)
                        evb = expT.sub(sk % LA)
                        p.op("t", lambda t, sk=sk, col0=col0, h=h, ev=ev, last=(sk == nsk - 1):
                             t.matmul(pb[nbk][0:65, col0:NSQ],
                                      vaug.ap[:, sk * H65 + h * 65: sk * H65 + h * 65 + 65],
                                      ev[:, col0:NSQ],
                                      start=(sk == 0), stop=last),
                             rd=[vaug.sub(sk), evb], wr=[pb_buf[nbk]])
                    LOOK = LA - 1
                    for sk in range(min(LOOK, nsk)):
                        emit_scores(sk)
                    for sk in range(nsk):
                        if sk + LOOK < nsk:
                            emit_scores(sk + LOOK)
                        emit_num(sk)
                    if (h * nsq_t + tq) % 2 == 0:
                        bg_pump(1)
                    dv = den.view((tq % 2) * 512, (tq % 2) * 512 + NSQ)
                    dvb = den.sub(tq % 2)
                    p.op("v", lambda v, dv=dv, nbk=nbk: v.reciprocal(dv[0:1, :], pb[nbk][64:65, 0:NSQ]),
                         rd=[pb_buf[nbk]], wr=[dvb])
                    p.op("t", lambda t, dv=dv, bbk=bbk: t.matmul(pb[bbk][0:64, 0:NSQ],
                                                        conesf.ap[0:1, 0:64], dv[0:1, :],
                                                        start=True, stop=True),
                         rd=[conesf.sub(0), dvb], wr=[pb_buf[bbk]])
                    nsb = xlnbf.view(par * 1024, par * 1024 + NSQ)
                    nsbb = xlnbf.sub(par)
                    p.op("s", lambda s, nsb=nsb, nbk=nbk: s.copy(nsb[0:64, :], pb[nbk][0:64, 0:NSQ]),
                         rd=[pb_buf[nbk]], wr=[nsbb])
                    p.op("v", lambda v, fc=fc, rows=rows, tq=tq, nsb=nsb, bbk=bbk:
                         v.tensor_tensor(aoT.ap[rows, fc * S + tq * NSQ: fc * S + (tq + 1) * NSQ],
                                         nsb[0:64, :], pb[bbk][0:64, 0:NSQ], ALU.mult),
                         rd=[nsbb, pb_buf[bbk]],
                         wr=[aoT.sub(fc * SC + tq * skpt + j) for j in range(skpt)])
            mem.free(qT); mem.free(kT); mem.free(vaug)
            for k in range(KD):
                p.dma("g", wnat.ap[:, k * D:(k + 1) * D],
                      w0["Wo"][k * 128:(k + 1) * 128, :], wr=[wnat.sub(0)])
            load_bias_tile(bias, b0_["bo"], D)
            res = mem.alloc("res0", F32, 2 * D, nsub=2)
            for c in range(SC):
                a, bf = xtile(c)
                p.dma("y", a, x_in[b, c * 128:(c + 1) * 128, :], wr=[bf])
                for nh in range(D // NTW):
                    bank = (c % 2) * 2 + (nh % 2)
                    for k in range(KD):
                        p.op("t", lambda t, bank=bank, k=k, c=c, nh=nh:
                             t.matmul(pb[bank][:, 0:NTW],
                                      aoT.ap[:, k * S + c * 128: k * S + c * 128 + 128],
                                      wnat.ap[:, k * D + nh * NTW: k * D + (nh + 1) * NTW],
                                      start=(k == 0), stop=(k == KD - 1)),
                             rd=[aoT.sub(k * SC + c), wnat.sub(0)], wr=[pb_buf[bank]])
                rv = res.view((c % 2) * D, (c % 2) * D + D)
                rvb = res.sub(c % 2)
                for nh in range(D // NTW):
                    bank = (c % 2) * 2 + (nh % 2)
                    p.op("v", lambda v, bank=bank, nh=nh, rv=rv:
                         v.scalar_tensor_tensor(rv[:, nh * NTW:(nh + 1) * NTW],
                                                pb[bank][:, 0:NTW], 1.0,
                                                bias.ap[:, nh * NTW:(nh + 1) * NTW],
                                                ALU.mult, ALU.add),
                         rd=[pb_buf[bank], bias.sub(0)], wr=[rvb])
                p.op("v", lambda v, rv=rv, a=a: v.tensor_tensor(rv, rv, a, ALU.add),
                     rd=[rvb, bf], wr=[rvb])
                emit_stats(rv, rvb, b * SC + c)
                p.dma("g", out_d[b, c * 128:(c + 1) * 128, :], rv,
                      rd=[rvb], wr=[out_bufs[b][c]])
            mem.free(aoT); mem.free(res); mem.free(xlnbf)
        mem.free(bias); mem.free(wnat); mem.free(expT); mem.free(den)

    FP4CW = 1024

    def emit_fp4(x_ap, x_bufs, dst_ap, dst_bufs, W, scr, CW=None):
        FP4CW = CW or 1024
        nb_all = W // 16
        sb = scr.sub(0)
        for c0 in range(0, W, FP4CW):
            c1 = min(W, c0 + FP4CW)
            cw = c1 - c0
            nb = cw // 16
            x3 = x_ap[:, c0:c1].rearrange("p (n k) -> p n k", k=16)
            bmv = scr.view(3 * FP4CW, 3 * FP4CW + nb_all)[:, c0 // 16: c1 // 16]
            rcpv = scr.view(3 * FP4CW + nb_all, 3 * FP4CW + 2 * nb_all)[:, c0 // 16: c1 // 16]
            a = scr.view(0, cw)
            q = scr.view(FP4CW, FP4CW + cw)
            bb = scr.view(2 * FP4CW, 2 * FP4CW + cw)
            bm3 = bmv.unsqueeze(2).broadcast_to([128, nb, 16])
            rcp3 = rcpv.unsqueeze(2).broadcast_to([128, nb, 16])
            d2 = dst_ap[:, c0:c1]
            rw = list(x_bufs) + [sb]
            p.op("v", lambda v, x3=x3, bmv=bmv: v.tensor_reduce(
                bmv.unsqueeze(2), x3, AX.X, ALU.max, apply_absolute_value=True),
                rd=list(x_bufs), wr=[sb])
            p.op("v", lambda v, bmv=bmv: v.tensor_scalar(
                bmv, bmv, 1.0 / 6.0, 1e-12, ALU.mult, ALU.max), rd=[sb], wr=[sb])
            p.op("v", lambda v, bmv=bmv, rcpv=rcpv: v.reciprocal(rcpv, bmv),
                 rd=[sb], wr=[sb])
            p.op("v", lambda v, x3=x3, rcp3=rcp3: v.tensor_tensor(
                x3, x3, rcp3, ALU.mult), rd=rw, wr=list(x_bufs))
            xv = x_ap[:, c0:c1]
            p.op("v", lambda v, a=a, xv=xv: v.scalar_tensor_tensor(
                a, xv, -1.0, xv, ALU.mult, ALU.max), rd=rw, wr=[sb])
            p.op("v", lambda v, a=a, q=q: v.tensor_scalar(
                q, a, 0.25, None, ALU.is_gt), rd=[sb], wr=[sb])
            for thr in (0.75, 1.25, 1.75):
                p.op("v", lambda v, a=a, q=q, thr=thr: v.scalar_tensor_tensor(
                    q, a, thr, q, ALU.is_gt, ALU.add), rd=[sb], wr=[sb])
            p.op("v", lambda v, a=a, bb=bb: v.tensor_scalar(
                bb, a, 2.5, None, ALU.is_gt), rd=[sb], wr=[sb])
            p.op("v", lambda v, a=a, bb=bb: v.scalar_tensor_tensor(
                bb, a, 3.5, bb, ALU.is_gt, ALU.add), rd=[sb], wr=[sb])
            p.op("v", lambda v, q=q, bb=bb: v.scalar_tensor_tensor(
                q, q, 0.5, bb, ALU.mult, ALU.add), rd=[sb], wr=[sb])
            p.op("v", lambda v, a=a, bb=bb: v.tensor_scalar(
                bb, a, 5.0, 2.0, ALU.is_gt, ALU.mult), rd=[sb], wr=[sb])
            p.op("v", lambda v, q=q, bb=bb: v.tensor_add(q, q, bb), rd=[sb], wr=[sb])
            p.op("s", lambda s, a=a, xv=xv: s.sign(a, xv), rd=rw, wr=[sb])
            p.op("v", lambda v, a=a, bm3=bm3, cw=cw: v.tensor_tensor(
                a.rearrange("p (n k) -> p n k", k=16), a.rearrange("p (n k) -> p n k", k=16),
                bm3, ALU.mult), rd=[sb], wr=[sb])
            p.op("v", lambda v, q=q, a=a, d2=d2: v.tensor_tensor(d2, q, a, ALU.mult),
                 rd=[sb], wr=list(dst_bufs))

    def quant_weight_fp4(src_dram, dst_dram, R, C, dbuf):
        wt = mem.alloc("wqt", F32, 2 * C, nsub=2)
        qt = mem.alloc("wqq", BF16, 2 * C, nsub=2)
        scr = mem.alloc("wqscr", F32, 3 * FP4CW + 2 * (C // 16))
        for j, r0 in enumerate(range(0, R, 128)):
            wv = wt.view((j % 2) * C, (j % 2) * C + C)
            qv = qt.view((j % 2) * C, (j % 2) * C + C)
            p.dma("y", wv, src_dram[r0:r0 + 128, :], wr=[wt.sub(j % 2)])
            emit_fp4(wv, [wt.sub(j % 2)], qv, [qt.sub(j % 2)], C, scr)
            p.dma("g", dst_dram[r0:r0 + 128, :], qv, rd=[qt.sub(j % 2)], wr=[dbuf])
        mem.free(wt); mem.free(qt); mem.free(scr)

    def emit_block1():
        nonlocal wtbg, scrbg
        bg_flush()
        mem.free(wtbg); mem.free(scrbg)
        finalize_stats(EPS5)
        NGU = max(1, (KD * IFFN) // 512)
        SUBW = (KD * IFFN) // NGU
        wgT = mem.alloc("wgT", BF16, KD * IFFN, nsub=NGU)
        wuT = mem.alloc("wuT", BF16, KD * IFFN, nsub=NGU)
        wdT = mem.alloc("wdT", BF16, IC * D)
        ntiles = [(n0, min(512, IFFN - n0)) for n0 in range(0, IFFN, 512)]
        for n0, nw in ntiles:
            for k in range(KD):
                for i in range(n0 // 128, (n0 + nw) // 128):
                    cc = k * IFFN + i * 128
                    dmat(wgT.ap[:, cc:cc + 128],
                         wq_gate_d[i * 128:(i + 1) * 128, k * 128:(k + 1) * 128],
                         rd=[wqd_bufs["g"]], wr=[wgT.sub(cc // SUBW)])
                    dmat(wuT.ap[:, cc:cc + 128],
                         wq_up_d[i * 128:(i + 1) * 128, k * 128:(k + 1) * 128],
                         rd=[wqd_bufs["u"]], wr=[wuT.sub(cc // SUBW)], q="s")
        for i in range(IC):
            for k in range(KD):
                dmat(wdT.ap[:, i * D + k * 128: i * D + (k + 1) * 128],
                     wq_down_d[k * 128:(k + 1) * 128, i * 128:(i + 1) * 128],
                     rd=[wqd_bufs["d"]], wr=[wdT.sub(0)])
        xq = mem.alloc("xq", BF16, D)
        xqT = mem.alloc("xqT", BF16, 2 * KD * 128, nsub=2)
        hqc = mem.alloc("hq", BF16, IFFN)
        hqT = mem.alloc("hqT", BF16, IC * 128)
        scr = mem.alloc("b1scr", F32, 3 * FP4CW + 2 * (max(D, IFFN) // 16))
        xlnf = mem.alloc("xlnf", F32, D)
        hf = mem.alloc("hf", F32, 2 * 512, nsub=2)
        sil = mem.alloc("sil", F32, 512)
        res = mem.alloc("res1", F32, 2 * D, nsub=2)
        dtiles = [(n0, min(512, D - n0)) for n0 in range(0, D, 512)]
        st = stats
        for b in range(NB):
            for c in range(SC):
                a, bf = xtile(c)
                p.dma("y", a, out_d[b, c * 128:(c + 1) * 128, :],
                      rd=[out_bufs[b][c]], wr=[bf])
                col = b * SC + c
                p.op("v", lambda v, a=a, col=col: v.tensor_scalar(
                    xlnf.ap, a, st["mu"].ap[:, col:col + 1], st["rstd"].ap[:, col:col + 1],
                    ALU.subtract, ALU.mult),
                    rd=[bf, st["mu"].sub(0), st["rstd"].sub(0)], wr=[xlnf.sub(0)])
                emit_fp4(xlnf.ap, [xlnf.sub(0)], xq.ap, [xq.sub(0)], D, scr)
                xqTo = (c % 2) * KD * 128
                for k in range(KD):
                    pet(xqT.ap[:, xqTo + k * 128: xqTo + (k + 1) * 128],
                        xq.ap[:, k * 128:(k + 1) * 128],
                        rd=[xq.sub(0)], wr=[xqT.sub(c % 2)])
                for nt, (n0, nw) in enumerate(ntiles):
                    bg, bu = nt % 2, 2 + nt % 2
                    for k in range(KD):
                        p.op("t", lambda t, bg=bg, k=k, n0=n0, nw=nw, xqTo=xqTo: t.matmul(
                            pb[bg][:, 0:nw], xqT.ap[:, xqTo + k * 128: xqTo + (k + 1) * 128],
                            wgT.ap[:, k * IFFN + n0: k * IFFN + n0 + nw],
                            start=(k == 0), stop=(k == KD - 1)),
                            rd=[xqT.sub(c % 2)] + [wgT.sub(s) for s in
                                range((k * IFFN + n0) // SUBW,
                                      (k * IFFN + n0 + nw - 1) // SUBW + 1)],
                            wr=[pb_buf[bg]])
                    for k in range(KD):
                        p.op("t", lambda t, bu=bu, k=k, n0=n0, nw=nw, xqTo=xqTo: t.matmul(
                            pb[bu][:, 0:nw], xqT.ap[:, xqTo + k * 128: xqTo + (k + 1) * 128],
                            wuT.ap[:, k * IFFN + n0: k * IFFN + n0 + nw],
                            start=(k == 0), stop=(k == KD - 1)),
                            rd=[xqT.sub(c % 2)] + [wuT.sub(s) for s in
                                range((k * IFFN + n0) // SUBW,
                                      (k * IFFN + n0 + nw - 1) // SUBW + 1)],
                            wr=[pb_buf[bu]])
                    p.op("s", lambda s, bg=bg, nw=nw: s.activation(
                        sil.ap[:, 0:nw], pb[bg][:, 0:nw], AF.Silu),
                        rd=[pb_buf[bg]], wr=[sil.sub(0)])
                    hv = hf.view((nt % 2) * 512, (nt % 2) * 512 + nw)
                    hvb = hf.sub(nt % 2)
                    p.op("v", lambda v, hv=hv, bu=bu, nw=nw: v.tensor_tensor(
                        hv, sil.ap[:, 0:nw], pb[bu][:, 0:nw], ALU.mult),
                        rd=[sil.sub(0), pb_buf[bu]], wr=[hvb])
                    emit_fp4(hv, [hvb], hqc.ap[:, n0:n0 + nw], [hqc.sub(0)], nw, scr)
                for i in range(IC):
                    pet(hqT.ap[:, i * 128:(i + 1) * 128], hqc.ap[:, i * 128:(i + 1) * 128],
                        rd=[hqc.sub(0)], wr=[hqT.sub(0)])
                rv = res.view((c % 2) * D, (c % 2) * D + D)
                rvb = res.sub(c % 2)
                for dn, (n0, nw) in enumerate(dtiles):
                    bank = 4 + dn % 2
                    for i in range(IC):
                        p.op("t", lambda t, bank=bank, i=i, n0=n0, nw=nw: t.matmul(
                            pb[bank][:, 0:nw], hqT.ap[:, i * 128:(i + 1) * 128],
                            wdT.ap[:, i * D + n0: i * D + n0 + nw],
                            start=(i == 0), stop=(i == IC - 1)),
                            rd=[hqT.sub(0), wdT.sub(0)], wr=[pb_buf[bank]])
                    p.op("v", lambda v, bank=bank, n0=n0, nw=nw, rv=rv, a=a:
                         v.tensor_tensor(rv[:, n0:n0 + nw], pb[bank][:, 0:nw],
                                         a[:, n0:n0 + nw], ALU.add),
                         rd=[pb_buf[bank], bf], wr=[rvb])
                emit_stats(rv, rvb, b * SC + c)
                p.dma("g", out_d[b, c * 128:(c + 1) * 128, :], rv,
                      rd=[rvb], wr=[out_bufs[b][c]])
        for r in (wgT, wuT, wdT, xq, xqT, hqc, hqT, scr, xlnf, hf, sil, res):
            mem.free(r)

    def emit_block2():
        nonlocal wtbg, scrbg
        wtbg = mem.alloc("wtbg2", F32, 1024)
        scrbg = mem.alloc("scrbg2", F32, 1024)
        queue_tern_tasks(w31, wq1_d, H2, D, wqd_bufs["1"], swb1)
        queue_tern_tasks(w32, wq2_d, D, H2, wqd_bufs["2"], swb2)
        finalize_stats(EPS5)
        st = stats
        projT = mem.alloc("projT", BF16, MC * 128)
        ptmp = mem.alloc("ptmp", BF16, 128)
        for mc in range(MC):
            p.dma("g", ptmp.ap[:, 0:64], proj_in[mc * 128:(mc + 1) * 128, :],
                  wr=[ptmp.sub(0)])
            p.dma("g", ptmp.ap[:, 64:128], proj_in[mc * 128:(mc + 1) * 128, :],
                  rd=[ptmp.sub(0)], wr=[ptmp.sub(0)])
            dmat(projT.ap[:, mc * 128:(mc + 1) * 128], ptmp.ap,
                 rd=[ptmp.sub(0)], wr=[projT.sub(0)])
        mem.free(ptmp)
        ones128 = mem.alloc("ones128", BF16, 1)
        p.op("v", lambda v: v.memset(ones128.ap, 1.0), wr=[ones128.sub(0)])
        gam = mem.alloc("gam", F32, D)
        bet = mem.alloc("bet", F32, D)
        load_bias_tile(gam, gamma_in, D)
        load_bias_tile(bet, beta_in, D)
        bias = mem.alloc("b2bias", F32, D)
        b2sq = mem.alloc("b2sq", F32, D)
        inner = mem.alloc("inner", F32, 8)
        qptmp = mem.alloc("qptmp", BF16, 2 * MFEAT, nsub=2)
        den2 = mem.alloc("den2", F32, 2 * 128, nsub=2)
        aoTp = mem.alloc("aoTp", BF16, S, nsub=SC)
        for b in range(NB):
            wnat = mem.alloc("b2w", BF16, KD * D)
            xlnT = mem.alloc("xlnT2", BF16, KD * S, nsub=SC)
            xlnbf = mem.alloc("xlnbf2", BF16, 2 * D, nsub=2)
            xlnf = mem.alloc("xlnf2", F32, D)
            qTa = mem.alloc("qTa", BF16, KD * S, nsub=KD * SC)
            kTa = mem.alloc("kTa", BF16, KD * S, nsub=KD * SC)
            vnat = mem.alloc("vnat", BF16, SC * D, nsub=SC)
            ncq = mem.alloc("ncq", F32, SC * HEADS)
            nck = mem.alloc("nck", F32, SC * HEADS)
            for c in range(SC):
                a, bf = xtile(c)
                p.dma("y", a, out_d[b, c * 128:(c + 1) * 128, :],
                      rd=[out_bufs[b][c]], wr=[bf])
                col = b * SC + c
                p.op("v", lambda v, a=a, col=col: v.tensor_scalar(
                    xlnf.ap, a, st["mu"].ap[:, col:col + 1], st["rstd"].ap[:, col:col + 1],
                    ALU.subtract, ALU.mult),
                    rd=[bf, st["mu"].sub(0), st["rstd"].sub(0)], wr=[xlnf.sub(0)])
                p.dma("g", xln_spill[b, c * 128:(c + 1) * 128, :], xlnf.ap,
                      rd=[xlnf.sub(0)], wr=[xsp_bufs[b][c]])
                xv = xlnbf.view((c % 2) * D, (c % 2 + 1) * D)
                xvb = xlnbf.sub(c % 2)
                p.op("v", lambda v, xv=xv: v.tensor_copy(xv, xlnf.ap),
                     rd=[xlnf.sub(0)], wr=[xvb])
                for k in range(KD):
                    pet(xlnT.ap[:, k * S + c * 128: k * S + c * 128 + 128],
                        xv[:, k * 128:(k + 1) * 128], rd=[xvb], wr=[xlnT.sub(c)])
            for ti, (wkey, bkey) in enumerate(
                    (("q_w", "q_b"), ("k_w", "k_b"), ("v_w", "v_b"))):
                for k in range(KD):
                    p.dma("g", wnat.ap[:, k * D:(k + 1) * D],
                          w2a[wkey][k * 128:(k + 1) * 128, :], wr=[wnat.sub(0)])
                load_bias_tile(bias, b2a[bkey], D)
                if ti < 2:
                    p.op("v", lambda v: v.tensor_scalar_mul(bias.ap[:, 0:D], bias.ap[:, 0:D], 0.125),
                         rd=[bias.sub(0)], wr=[bias.sub(0)])
                qsc = 0.125 if ti < 2 else 1.0
                tgt = (qTa, kTa, None)[ti]
                ncd = (ncq, nck, None)[ti]
                for c in range(SC):
                    if ti < 2:
                        dst = xlnbf.view((c % 2) * D, (c % 2 + 1) * D)
                        dstb = xlnbf.sub(c % 2)
                    for nh in range(D // NTW):
                        bank = (c % 2) * 2 + (nh % 2)
                        for k in range(KD):
                            p.op("t", lambda t, bank=bank, k=k, c=c, nh=nh:
                                 t.matmul(pb[bank][:, 0:NTW],
                                          xlnT.ap[:, k * S + c * 128: k * S + c * 128 + 128],
                                          wnat.ap[:, k * D + nh * NTW: k * D + (nh + 1) * NTW],
                                          start=(k == 0), stop=(k == KD - 1)),
                                 rd=[xlnT.sub(c), wnat.sub(0)], wr=[pb_buf[bank]])
                        if ti < 2:
                            p.op("v", lambda v, bank=bank, nh=nh, dst=dst, qsc=qsc:
                                 v.scalar_tensor_tensor(
                                     dst[:, nh * NTW:(nh + 1) * NTW], pb[bank][:, 0:NTW],
                                     qsc, bias.ap[:, nh * NTW:(nh + 1) * NTW],
                                     ALU.mult, ALU.add),
                                 rd=[pb_buf[bank], bias.sub(0)], wr=[dstb])
                        else:
                            p.op("v", lambda v, bank=bank, nh=nh, c=c:
                                 v.scalar_tensor_tensor(
                                     vnat.ap[:, c * D + nh * NTW: c * D + (nh + 1) * NTW],
                                     pb[bank][:, 0:NTW], 1.0,
                                     bias.ap[:, nh * NTW:(nh + 1) * NTW], ALU.mult, ALU.add),
                                 rd=[pb_buf[bank], bias.sub(0)], wr=[vnat.sub(c)])
                    if ti < 2:
                        p.op("v", lambda v, dst=dst: v.tensor_mul(b2sq.ap, dst, dst),
                             rd=[dstb], wr=[b2sq.sub(0)])
                        p.op("v", lambda v, c=c, ncd=ncd: v.tensor_reduce(
                            ncd.ap[:, c * HEADS:(c + 1) * HEADS].unsqueeze(2),
                            b2sq.ap.rearrange("p (h e) -> p h e", e=64), AX.X, ALU.add),
                            rd=[b2sq.sub(0)], wr=[ncd.sub(0)])
                        p.op("v", lambda v, c=c, ncd=ncd: v.tensor_scalar_mul(
                            ncd.ap[:, c * HEADS:(c + 1) * HEADS],
                            ncd.ap[:, c * HEADS:(c + 1) * HEADS], -0.5),
                            rd=[ncd.sub(0)], wr=[ncd.sub(0)])
                        for k in range(KD):
                            pet(tgt.ap[:, k * S + c * 128: k * S + c * 128 + 128],
                                dst[:, k * 128:(k + 1) * 128],
                                rd=[dstb], wr=[tgt.sub(k * SC + c)])
                    bg_pump(1)
            mem.free(xlnT); mem.free(xlnbf); mem.free(xlnf); mem.free(wnat)
            hsets = []
            for hl in range(2):
                hsets.append(dict(
                    qpT=mem.alloc(f"qpT{hl}", BF16, MC * S),
                    kpT=mem.alloc(f"kpT{hl}", BF16, MC * S),
                    kpn=mem.alloc(f"kpn{hl}", BF16, SC * MFEAT, nsub=SC),
                    KVs=mem.alloc(f"KVs{hl}", BF16, MC * 65),
                    KVf=mem.alloc(f"KVf{hl}", F32, MC * 65),
                    aT=mem.alloc(f"aT{hl}", BF16, 128),
                ))
            NBANK = (5, 1)
            ABANK = (2, 3)
            KVOFF = (0, 130)
            for fc in range(HEADS // 2):
                for hl in range(2):
                    hs = hsets[hl]
                    h = 2 * fc + hl
                    rows = slice(hl * 64, hl * 64 + 64)
                    p.op("v", lambda v, hs=hs: v.memset(hs["KVf"].ap, 0.0),
                         wr=[hs["KVf"].sub(0)])
                    for ti2, (srcT, ncd) in enumerate(((qTa, ncq), (kTa, nck))):
                        for c in range(SC):
                            bank = c % 2
                            p.op("t", lambda t, bank=bank, c=c, srcT=srcT, rows=rows, fc=fc:
                                 t.matmul(pb[bank][:, 0:MFEAT],
                                          srcT.ap[rows, fc * S + c * 128: fc * S + (c + 1) * 128],
                                          projT.ap[rows, :],
                                          start=True, stop=True),
                                 rd=[srcT.sub(fc * SC + c), projT.sub(0)], wr=[pb_buf[bank]])
                            if ti2 == 0:
                                qv = qptmp.view((c % 2) * MFEAT, (c % 2) * MFEAT + MFEAT)
                                qvb = qptmp.sub(c % 2)
                                p.op("s", lambda s, bank=bank, c=c, qv=qv, ncd=ncd, h=h:
                                     s.activation(qv, pb[bank][:, 0:MFEAT], AF.Exp,
                                                  bias=ncd.ap[:, c * HEADS + h: c * HEADS + h + 1]),
                                     rd=[pb_buf[bank], ncd.sub(0)], wr=[qvb])
                                for mc in range(MC):
                                    pet(hs["qpT"].ap[:, mc * S + c * 128: mc * S + (c + 1) * 128],
                                        qv[:, mc * 128:(mc + 1) * 128],
                                        rd=[qvb], wr=[hs["qpT"].sub(0)])
                            else:
                                kslice = hs["kpn"].ap[:, c * MFEAT:(c + 1) * MFEAT]
                                p.op("s", lambda s, bank=bank, c=c, kslice=kslice, ncd=ncd, h=h:
                                     s.activation(kslice, pb[bank][:, 0:MFEAT], AF.Exp,
                                                  bias=ncd.ap[:, c * HEADS + h: c * HEADS + h + 1]),
                                     rd=[pb_buf[bank], ncd.sub(0)], wr=[hs["kpn"].sub(c)])
                                for mc in range(MC):
                                    pet(hs["kpT"].ap[:, mc * S + c * 128: mc * S + (c + 1) * 128],
                                        kslice[:, mc * 128:(mc + 1) * 128],
                                        rd=[hs["kpn"].sub(c)], wr=[hs["kpT"].sub(0)])
                for ci in range(SC):
                    for hl in range(2):
                        hs = hsets[hl]
                        h = 2 * fc + hl
                        rows = slice(hl * 64, hl * 64 + 64)
                        nb_, ab, kvo = NBANK[hl], ABANK[hl], KVOFF[hl]
                        qpT, kpT, kpn = hs["qpT"], hs["kpT"], hs["kpn"]
                        KVs, KVf, aT = hs["KVs"], hs["KVf"], hs["aT"]
                        vs = vnat.ap[:, ci * D + fc * 128 + hl * 64:
                                     ci * D + fc * 128 + hl * 64 + 64]
                        first = ci == 0
                        if not first:
                            for mc in range(MC):
                                p.op("t", lambda t, mc=mc, ci=ci, nb_=nb_, qpT=qpT, KVs=KVs: t.matmul(
                                    pb[nb_][0:64, 0:128],
                                    KVs.ap[:, mc * 65: mc * 65 + 64],
                                    qpT.ap[:, mc * S + ci * 128: mc * S + (ci + 1) * 128],
                                    start=(mc == 0), stop=False, skip_group_check=True),
                                    rd=[KVs.sub(0), qpT.sub(0)], wr=[pb_buf[nb_]])
                                p.op("t", lambda t, mc=mc, ci=ci, nb_=nb_, qpT=qpT, KVs=KVs: t.matmul(
                                    pb[nb_][64:65, 0:128],
                                    KVs.ap[:, mc * 65 + 64: mc * 65 + 65],
                                    qpT.ap[:, mc * S + ci * 128: mc * S + (ci + 1) * 128],
                                    start=(mc == 0), stop=False, skip_group_check=True),
                                    rd=[KVs.sub(0), qpT.sub(0)], wr=[pb_buf[nb_]])
                        for mc in range(MC):
                            p.op("t", lambda t, mc=mc, ci=ci, ab=ab, kpT=kpT, qpT=qpT: t.matmul(
                                pb[ab][:, 0:128],
                                kpT.ap[:, mc * S + ci * 128: mc * S + (ci + 1) * 128],
                                qpT.ap[:, mc * S + ci * 128: mc * S + (ci + 1) * 128],
                                start=(mc == 0), stop=(mc == MC - 1)),
                                rd=[kpT.sub(0), qpT.sub(0)], wr=[pb_buf[ab]])
                        p.op("v", lambda v, ab=ab, aT=aT: v.tensor_tensor(
                            aT.ap, pb[ab][:, 0:128], cmask.ap, ALU.mult),
                            rd=[pb_buf[ab], cmask.sub(0)], wr=[aT.sub(0)])
                        p.op("t", lambda t, vs=vs, first=first, nb_=nb_, aT=aT: t.matmul(
                            pb[nb_][0:64, 0:128], vs, aT.ap,
                            start=first, stop=True, skip_group_check=True),
                            rd=[vnat.sub(ci), aT.sub(0)], wr=[pb_buf[nb_]])
                        p.op("t", lambda t, first=first, nb_=nb_, aT=aT: t.matmul(
                            pb[nb_][64:65, 0:128], ones128.ap, aT.ap,
                            start=first, stop=True, skip_group_check=True),
                            rd=[ones128.sub(0), aT.sub(0)], wr=[pb_buf[nb_]])
                        for mc in range(MC):
                            p.op("t", lambda t, mc=mc, ci=ci, vs=vs, kvo=kvo, kpn=kpn: t.matmul(
                                pb[4][:, kvo + mc * 65: kvo + mc * 65 + 64],
                                kpn.ap[:, ci * MFEAT + mc * 128: ci * MFEAT + (mc + 1) * 128],
                                vs, start=True, stop=True, skip_group_check=True),
                                rd=[kpn.sub(ci), vnat.sub(ci)], wr=[pb_buf[4]])
                            p.op("t", lambda t, mc=mc, ci=ci, kvo=kvo, kpn=kpn: t.matmul(
                                pb[4][:, kvo + mc * 65 + 64: kvo + mc * 65 + 65],
                                kpn.ap[:, ci * MFEAT + mc * 128: ci * MFEAT + (mc + 1) * 128],
                                ones128.ap, start=True, stop=True, skip_group_check=True),
                                rd=[kpn.sub(ci), ones128.sub(0)], wr=[pb_buf[4]])
                        for mc in range(MC):
                            kvv = KVf.view(mc * 65, (mc + 1) * 65)
                            p.op("v", lambda v, mc=mc, kvv=kvv, kvo=kvo: v.tensor_tensor(
                                kvv, kvv, pb[4][:, kvo + mc * 65: kvo + (mc + 1) * 65], ALU.add),
                                rd=[KVf.sub(0), pb_buf[4]], wr=[KVf.sub(0)])
                            p.op("v", lambda v, mc=mc, kvv=kvv, KVs=KVs: v.tensor_copy(
                                KVs.ap[:, mc * 65:(mc + 1) * 65], kvv),
                                rd=[KVf.sub(0)], wr=[KVs.sub(0)])
                        dv = den2.view(hl * 128, hl * 128 + 128)
                        dvb = den2.sub(hl)
                        nsb2 = qptmp.view(hl * MFEAT, hl * MFEAT + 128)
                        nsb2b = qptmp.sub(hl)
                        p.op("v", lambda v, nb_=nb_, dv=dv: v.reciprocal(dv[0:1, :], pb[nb_][64:65, 0:128]),
                             rd=[pb_buf[nb_]], wr=[dvb])
                        p.op("s", lambda s, nb_=nb_, nsb2=nsb2: s.copy(nsb2[0:64, :], pb[nb_][0:64, 0:128]),
                             rd=[pb_buf[nb_]], wr=[nsb2b])
                        p.op("t", lambda t, nb_=nb_, dv=dv: t.matmul(pb[nb_][0:64, 0:128],
                                                     conesf.ap[0:1, 0:64], dv[0:1, :],
                                                     start=True, stop=True, skip_group_check=True),
                             rd=[conesf.sub(0), dvb], wr=[pb_buf[nb_]])
                        p.op("v", lambda v, ci=ci, rows=rows, nb_=nb_, nsb2=nsb2: v.tensor_tensor(
                            aoTp.ap[rows, ci * 128:(ci + 1) * 128],
                            nsb2[0:64, :], pb[nb_][0:64, 0:128], ALU.mult),
                            rd=[nsb2b, pb_buf[nb_]], wr=[aoTp.sub(ci)])
                p.dma("g", att_spill[b, fc * 128:(fc + 1) * 128, :], aoTp.ap,
                      rd=list(aoTp.all_bufs()), wr=[asp_bufs[b][fc]])
                bg_pump(1)
            for hs in hsets:
                for r in hs.values():
                    mem.free(r)
            mem.free(qTa); mem.free(kTa); mem.free(vnat)
            mem.free(ncq); mem.free(nck)
            wnat = mem.alloc("b2w", BF16, KD * D)
            for k in range(KD):
                p.dma("g", wnat.ap[:, k * D:(k + 1) * D],
                      w2a["out_w"][k * 128:(k + 1) * 128, :], wr=[wnat.sub(0)])
            load_bias_tile(bias, b2a["out_b"], D)
            aol = mem.alloc("aol", BF16, 2 * KD * 128, nsub=2)
            res = mem.alloc("res2", F32, 2 * D, nsub=2)
            for c in range(SC):
                alo = (c % 2) * KD * 128
                for k in range(KD):
                    p.dma("y", aol.ap[:, alo + k * 128: alo + (k + 1) * 128],
                          att_spill[b, k * 128:(k + 1) * 128, c * 128:(c + 1) * 128],
                          rd=[asp_bufs[b][k]], wr=[aol.sub(c % 2)])
                for nh in range(D // NTW):
                    bank = (c % 2) * 2 + (nh % 2)
                    for k in range(KD):
                        p.op("t", lambda t, bank=bank, k=k, nh=nh, alo=alo: t.matmul(
                            pb[bank][:, 0:NTW], aol.ap[:, alo + k * 128: alo + (k + 1) * 128],
                            wnat.ap[:, k * D + nh * NTW: k * D + (nh + 1) * NTW],
                            start=(k == 0), stop=(k == KD - 1)),
                            rd=[aol.sub(c % 2), wnat.sub(0)], wr=[pb_buf[bank]])
                rv = res.view((c % 2) * D, (c % 2) * D + D)
                rvb = res.sub(c % 2)
                for nh in range(D // NTW):
                    bank = (c % 2) * 2 + (nh % 2)
                    p.op("v", lambda v, bank=bank, nh=nh, rv=rv: v.scalar_tensor_tensor(
                        rv[:, nh * NTW:(nh + 1) * NTW], pb[bank][:, 0:NTW], 1.0,
                        bias.ap[:, nh * NTW:(nh + 1) * NTW], ALU.mult, ALU.add),
                        rd=[pb_buf[bank], bias.sub(0)], wr=[rvb])
                a, bf = xtile(c)
                p.dma("y", a, xln_spill[b, c * 128:(c + 1) * 128, :],
                      rd=[xsp_bufs[b][c]], wr=[bf])
                p.op("v", lambda v, rv=rv, a=a: v.tensor_tensor(rv, rv, a, ALU.add),
                     rd=[rvb, bf], wr=[rvb])
                p.op("v", lambda v, rv=rv: v.reduce_sum(inner.ap[:, 0:1], rv, AX.X),
                     rd=[rvb], wr=[inner.sub(0)])
                p.op("s", lambda s, rv=rv: s.activation(
                    sq_scratch.ap[:, 0:D], rv, AF.Square, accum_out=inner.ap[:, 1:2]),
                    rd=[rvb], wr=[sq_scratch.sub(0), inner.sub(0)])
                p.op("v", lambda v: v.tensor_scalar_mul(inner.ap[:, 2:3], inner.ap[:, 0:1], 1.0 / D),
                     rd=[inner.sub(0)], wr=[inner.sub(0)])
                p.op("v", lambda v: v.tensor_scalar_mul(inner.ap[:, 3:4], inner.ap[:, 1:2], 1.0 / D),
                     rd=[inner.sub(0)], wr=[inner.sub(0)])
                p.op("v", lambda v: v.tensor_mul(inner.ap[:, 4:5], inner.ap[:, 2:3], inner.ap[:, 2:3]),
                     rd=[inner.sub(0)], wr=[inner.sub(0)])
                p.op("v", lambda v: v.tensor_sub(inner.ap[:, 3:4], inner.ap[:, 3:4], inner.ap[:, 4:5]),
                     rd=[inner.sub(0)], wr=[inner.sub(0)])
                p.op("s", lambda s: s.activation(inner.ap[:, 5:6], inner.ap[:, 3:4],
                                                 AF.Sqrt, bias=EPS6),
                     rd=[inner.sub(0), consts.sub(0)], wr=[inner.sub(0)])
                p.op("v", lambda v: v.reciprocal(inner.ap[:, 5:6], inner.ap[:, 5:6]),
                     rd=[inner.sub(0)], wr=[inner.sub(0)])
                p.op("v", lambda v, rv=rv: v.tensor_scalar(
                    rv, rv, inner.ap[:, 2:3], inner.ap[:, 5:6], ALU.subtract, ALU.mult),
                    rd=[rvb, inner.sub(0)], wr=[rvb])
                p.op("v", lambda v, rv=rv: v.tensor_mul(rv, rv, gam.ap),
                     rd=[rvb, gam.sub(0)], wr=[rvb])
                p.op("v", lambda v, rv=rv: v.tensor_add(rv, rv, bet.ap),
                     rd=[rvb, bet.sub(0)], wr=[rvb])
                a2t, bf2 = xtile(c + 1)
                p.dma("y", a2t, out_d[b, c * 128:(c + 1) * 128, :],
                      rd=[out_bufs[b][c]], wr=[bf2])
                p.op("v", lambda v, rv=rv, a2t=a2t: v.tensor_tensor(rv, rv, a2t, ALU.add),
                     rd=[rvb, bf2], wr=[rvb])
                emit_stats(rv, rvb, b * SC + c)
                p.dma("g", out_d[b, c * 128:(c + 1) * 128, :], rv,
                      rd=[rvb], wr=[out_bufs[b][c]])
                bg_pump(2)
            mem.free(aol); mem.free(res); mem.free(wnat)
        for r in (projT, ones128, gam, bet, bias, b2sq, inner, qptmp, den2, aoTp):
            mem.free(r)

    def emit_round_q(src_ap, src_bufs, scale_ap, scale_bufs, dst_ap, dst_bufs,
                     W, rscr, iscr):
        for j0 in range(0, W, 1024):
            jw = min(1024, W - j0)
            yp = rscr.view(2048, 2048 + jw)
            r0f = rscr.view(0, jw)
            cb = rscr.view(1024, 1024 + jw)
            it = iscr.view(0, jw)
            d = dst_ap[:, j0:j0 + jw]
            src_pc = src_ap[:, j0:j0 + jw]
            sb = rscr.sub(0)
            ib = iscr.sub(0)
            s_bufs = src_bufs(j0, jw) if callable(src_bufs) else list(src_bufs)
            d_bufs = dst_bufs(j0, jw) if callable(dst_bufs) else list(dst_bufs)
            p.op("s", lambda s, yp=yp, src_pc=src_pc: s.activation(
                yp, src_pc, AF.Identity, bias=C2565, scale=scale_ap),
                rd=s_bufs + list(scale_bufs) + [consts.sub(0)], wr=[sb])
            p.op("v", lambda v, it=it, yp=yp: v.tensor_copy(it, yp),
                 rd=[sb], wr=[ib])
            p.op("v", lambda v, r0f=r0f, it=it: v.tensor_single_scalar(
                r0f, it, -264.0, ALU.add), rd=[ib], wr=[sb])
            p.op("v", lambda v, yp=yp, r0f=r0f: v.tensor_tensor(
                yp, yp, r0f, ALU.subtract), rd=[sb], wr=[sb])
            for thr, w in ((264.0, 8.0), (260.0, 4.0), (258.0, 2.0), (257.0, 1.0)):
                p.op("v", lambda v, cb=cb, yp=yp, thr=thr: v.tensor_single_scalar(
                    cb, yp, thr, ALU.is_ge), rd=[sb], wr=[sb])
                acc, accb = (r0f, [sb]) if w == 8.0 else (d, d_bufs)
                p.op("v", lambda v, d=d, cb=cb, acc=acc, w=w: v.scalar_tensor_tensor(
                    d, cb, w, acc, ALU.mult, ALU.add),
                    rd=[sb] + accb, wr=d_bufs)
                if w != 1.0:
                    p.op("v", lambda v, yp=yp, cb=cb, w=w: v.scalar_tensor_tensor(
                        yp, cb, -w, yp, ALU.mult, ALU.add), rd=[sb], wr=[sb])

    def quant_weight_tern(src_dram, dst_dram, R, C, dbuf, onescol):
        wt = mem.alloc("twt", F32, C)
        qt = mem.alloc("twq", BF16, C)
        acc = mem.alloc("tacc", F32, 4)
        scr = mem.alloc("tscr", F32, 1024)
        out = mem.alloc("swb", F32, 2)
        p.op("v", lambda v: v.memset(acc.ap[:, 0:1], 0.0), wr=[acc.sub(0)])
        for r0 in range(0, R, 128):
            p.dma("y", wt.ap, src_dram[r0:r0 + 128, :], wr=[wt.sub(0)])
            p.op("v", lambda v: v.tensor_reduce(
                acc.ap[:, 1:2], wt.ap, AX.X, ALU.add, apply_absolute_value=True),
                rd=[wt.sub(0)], wr=[acc.sub(0)])
            p.op("v", lambda v: v.tensor_add(acc.ap[:, 0:1], acc.ap[:, 0:1], acc.ap[:, 1:2]),
                 rd=[acc.sub(0)], wr=[acc.sub(0)])
        p.op("t", lambda t: t.matmul(pb[5][0:1, 0:1], onescol.ap, acc.ap[:, 0:1],
                                     start=True, stop=True, skip_group_check=True),
             rd=[onescol.sub(0), acc.sub(0)], wr=[pb_buf[5]])
        p.op("v", lambda v, n=R * C: v.tensor_scalar(
            acc.ap[0:1, 2:3], pb[5][0:1, 0:1], 1.0 / n, 1e-5, ALU.mult, ALU.max),
            rd=[pb_buf[5]], wr=[acc.sub(0)])
        p.op("v", lambda v: v.reciprocal(acc.ap[0:1, 3:4], acc.ap[0:1, 2:3]),
             rd=[acc.sub(0)], wr=[acc.sub(0)])
        p.op("t", lambda t: t.matmul(pb[5][0:128, 0:2], conesf.ap[0:1, 0:128],
                                     acc.ap[0:1, 2:4], start=True, stop=True),
             rd=[conesf.sub(0), acc.sub(0)], wr=[pb_buf[5]])
        p.op("v", lambda v: v.tensor_copy(out.ap, pb[5][0:128, 0:2]),
             rd=[pb_buf[5]], wr=[out.sub(0)])
        for r0 in range(0, R, 128):
            p.dma("y", wt.ap, src_dram[r0:r0 + 128, :], wr=[wt.sub(0)])
            p.op("v", lambda v: v.tensor_scalar_mul(
                wt.ap, wt.ap, out.ap[:, 1:2]),
                rd=[wt.sub(0), out.sub(0)], wr=[wt.sub(0)])
            for j0 in range(0, C, 1024):
                jw = min(1024, C - j0)
                p.op("v", lambda v, j0=j0, jw=jw: v.tensor_single_scalar(
                    scr.ap[:, 0:jw], wt.ap[:, j0:j0 + jw], -0.5, ALU.is_lt),
                    rd=[wt.sub(0)], wr=[scr.sub(0)])
                p.op("v", lambda v, j0=j0, jw=jw: v.scalar_tensor_tensor(
                    qt.ap[:, j0:j0 + jw], wt.ap[:, j0:j0 + jw], 0.5,
                    scr.ap[:, 0:jw], ALU.is_gt, ALU.subtract),
                    rd=[wt.sub(0), scr.sub(0)], wr=[qt.sub(0)])
            p.dma("g", dst_dram[r0:r0 + 128, :], qt.ap, rd=[qt.sub(0)], wr=[dbuf])
        for r in (wt, qt, acc, scr):
            mem.free(r)
        return out

    def emit_block3():
        bg_flush()
        mem.free(wtbg); mem.free(scrbg)
        sw1, sw2 = swb1, swb2
        finalize_stats(EPS5)
        st = stats
        rr = mem.alloc("rr", F32, NSC)
        p.op("v", lambda v: v.tensor_mul(rr.ap, st["rstd"].ap, st["rstd"].ap),
             rd=[st["rstd"].sub(0)], wr=[rr.sub(0)])
        p.op("v", lambda v: v.tensor_mul(rr.ap, rr.ap, st["var"].ap),
             rd=[rr.sub(0), st["var"].sub(0)], wr=[rr.sub(0)])
        p.op("s", lambda s: s.activation(rr.ap, rr.ap, AF.Sqrt, bias=EPS6),
             rd=[rr.sub(0), consts.sub(0)], wr=[rr.sub(0)])
        p.op("v", lambda v: v.reciprocal(rr.ap, rr.ap), rd=[rr.sub(0)], wr=[rr.sub(0)])
        p.op("v", lambda v: v.tensor_mul(rr.ap, rr.ap, st["rstd"].ap),
             rd=[rr.sub(0), st["rstd"].sub(0)], wr=[rr.sub(0)])
        BCW = min(2048, H2)
        PB1 = H2 // BCW
        assert PB1 <= 2 and D <= BCW
        NP1 = 32 * (PB1 - 1) + 1
        h2tiles = [(n0, min(512, H2 - n0)) for n0 in range(0, H2, 512)]
        dtiles = [(n0, min(512, D - n0)) for n0 in range(0, D, 512)]
        nparts = H2 // D
        NH5 = H2 // 512
        wq1T = mem.alloc("wq1T", BF16, KD * H2, nsub=KD * NH5)
        wq2T = mem.alloc("wq2T", BF16, H2C * D)
        for nt in range(NH5):
            for k in range(KD):
                for i in range(nt * 4, min(nt * 4 + 4, H2C)):
                    dmat(wq1T.ap[:, k * H2 + i * 128: k * H2 + (i + 1) * 128],
                         wq1_d[i * 128:(i + 1) * 128, k * 128:(k + 1) * 128],
                         rd=[wqd_bufs["1"]], wr=[wq1T.sub(k * NH5 + nt)])
        for i in range(H2C):
            for k in range(KD):
                dmat(wq2T.ap[:, i * D + k * 128: i * D + (k + 1) * 128],
                     wq2_d[k * 128:(k + 1) * 128, i * 128:(i + 1) * 128],
                     rd=[wqd_bufs["2"]], wr=[wq2T.sub(0)], q="s")
        browT = mem.alloc("browT", BF16, BCW + 384)
        for pp in range(PB1):
            p.dma("g", browT.ap[32 * pp:32 * pp + 1, 0:BCW],
                  b31[pp * BCW:(pp + 1) * BCW].unsqueeze(0), wr=[browT.sub(0)])
        p.dma("g", browT.ap[64:65, 0:D], b32.unsqueeze(0), wr=[browT.sub(0)])
        ivrow = mem.alloc("ivrow", BF16, 128)
        ivb = mem.alloc("ivb", BF16, 4)
        ivf = mem.alloc("ivf", F32, 4)
        xq1 = mem.alloc("xq1", BF16, 2 * D, nsub=2)
        xq1T = mem.alloc("xq1T", BF16, KD * 128)
        hth = mem.alloc("hth", BF16, H2, nsub=H2 // 512)
        xq2T = mem.alloc("xq2T", BF16, H2C * 128)
        xnf = mem.alloc("xnf", F32, D)
        res = mem.alloc("res3", F32, 2 * D, nsub=2)
        rowm = mem.alloc("rowm", F32, 16, nsub=2)
        modscr = mem.alloc("modscr", F32, 3 * 1024)
        iscr = mem.alloc("iscr", I32, 1024)
        msc1 = mem.alloc("msc1", F32, NSC)
        msc2 = mem.alloc("msc2", F32, NSC)

        def iv_to_row(msc_col, msc_bufs, sl, which, g):
            iv_c = ivf.ap[:, which * 2 + (g % 2): which * 2 + (g % 2) + 1]
            ivb_c = ivb.ap[:, which * 2 + (g % 2): which * 2 + (g % 2) + 1]
            p.op("v", lambda v: v.reciprocal(iv_c, msc_col),
                 rd=list(msc_bufs), wr=[ivf.sub(0)])
            p.op("s", lambda s: s.copy(ivb_c, iv_c), rd=[ivf.sub(0)], wr=[ivb.sub(0)])
            ti = tp_ctr[0] & 1
            tp_ctr[0] += 1
            p.op("t", lambda t, ti=ti: t.transpose(tp[ti][0:1, 0:128], ivb_c, cident.ap),
                 rd=[ivb.sub(0), cident.sub(0)], wr=[tp_buf[ti]])
            rvw = ivrow.ap[0:1, 0:128]
            p.op("s", lambda s, ti=ti: s.copy(rvw, tp[ti][0:1, 0:128]),
                 rd=[tp_buf[ti]], wr=[ivrow.sub(0)])
            pbase, pn = (0, NP1) if which == 0 else (64, 1)
            p.op("t", lambda t: t.matmul(pb[4][pbase:pbase + pn, 0:128],
                                         cmask.ap[0:1, 0:pn], rvw,
                                         start=True, stop=True, skip_group_check=True),
                 rd=[ivrow.sub(0), cmask.sub(0)], wr=[pb_buf[4]])
            dstc = BCW if which == 0 else BCW + 128 + sl * 128
            p.op("s", lambda s: s.copy(browT.ap[pbase:pbase + pn, dstc:dstc + 128],
                                       pb[4][pbase:pbase + pn, 0:128]),
                 rd=[pb_buf[4]], wr=[browT.sub(0)])

        def emit_pets2(g):
            for i in range(H2C):
                pet(xq2T.ap[:, i * 128:(i + 1) * 128],
                    hth.ap[:, i * 128:(i + 1) * 128],
                    rd=[hth.sub(i // 4)], wr=[xq2T.sub(0)])

        def emit_mm2(g):
            b, c = g // SC, g % SC
            sl2 = g % 2
            rv = res.view(sl2 * D, sl2 * D + D)
            rvb = res.sub(sl2)
            a, bf = xtile(g)
            for dn, (n0, nw) in enumerate(dtiles):
                bank = 2 + dn % 2
                p2i = 64
                bc2 = BCW + 128 + sl2 * 128
                for i in range(H2C):
                    p.op("t", lambda t, bank=bank, i=i, n0=n0, nw=nw: t.matmul(
                        pb[bank][:, 0:nw], xq2T.ap[:, i * 128:(i + 1) * 128],
                        wq2T.ap[:, i * D + n0: i * D + n0 + nw],
                        start=(i == 0), stop=(i == H2C - 1)),
                        rd=[xq2T.sub(0), wq2T.sub(0)], wr=[pb_buf[bank]])
                    if i == 0:
                        p.op("t", lambda t, bank=bank, n0=n0, nw=nw, p2i=p2i, bc2=bc2: t.matmul(
                            pb[bank][:, 0:nw], browT.ap[p2i:p2i + 1, bc2:bc2 + 128],
                            browT.ap[p2i:p2i + 1, n0 % BCW: n0 % BCW + nw],
                            start=False, stop=False, skip_group_check=True),
                            rd=[browT.sub(0)], wr=[pb_buf[bank]])
                p.op("s", lambda s, bank=bank, n0=n0, nw=nw, rv=rv, g=g: s.activation(
                    rv[:, n0:n0 + nw], pb[bank][:, 0:nw], AF.Copy, bias=0.0,
                    scale=msc2.ap[:, g:g + 1]),
                    rd=[pb_buf[bank], msc2.sub(0)], wr=[rvb])
            p.op("v", lambda v, rv=rv, a=a: v.tensor_tensor(rv, rv, a, ALU.add),
                 rd=[rvb, bf], wr=[rvb])
            p.dma("g", out_d[b, c * 128:(c + 1) * 128, :], rv,
                  rd=[rvb], wr=[out_bufs[b][c]])

        for g in range(NB * SC):
            b, c = g // SC, g % SC
            sl1 = g % 2
            ro = sl1 * 8
            a, bf = xtile(g)
            p.dma("s", a, out_d[b, c * 128:(c + 1) * 128, :],
                  rd=[out_bufs[b][c]], wr=[bf])
            p.op("v", lambda v, a=a, g=g: v.tensor_scalar(
                xnf.ap, a, st["mu"].ap[:, g:g + 1], rr.ap[:, g:g + 1],
                ALU.subtract, ALU.mult),
                rd=[bf, st["mu"].sub(0), rr.sub(0)], wr=[xnf.sub(0)])
            p.op("v", lambda v, ro=ro: v.tensor_reduce(
                rowm.ap[:, ro:ro + 1], xnf.ap, AX.X, ALU.max, apply_absolute_value=True),
                rd=[xnf.sub(0)], wr=[rowm.sub(sl1)])
            p.op("v", lambda v, ro=ro: v.tensor_scalar_max(
                rowm.ap[:, ro:ro + 1], rowm.ap[:, ro:ro + 1], 1e-5),
                rd=[rowm.sub(sl1)], wr=[rowm.sub(sl1)])
            p.op("v", lambda v, ro=ro: v.reciprocal(
                rowm.ap[:, ro + 1:ro + 2], rowm.ap[:, ro:ro + 1]),
                rd=[rowm.sub(sl1)], wr=[rowm.sub(sl1)])
            p.op("v", lambda v, ro=ro: v.tensor_scalar_mul(
                rowm.ap[:, ro + 1:ro + 2], rowm.ap[:, ro + 1:ro + 2], 127.0),
                rd=[rowm.sub(sl1)], wr=[rowm.sub(sl1)])
            emit_round_q(xnf.ap, [xnf.sub(0)], rowm.ap[:, ro + 1:ro + 2],
                         [rowm.sub(sl1)], xq1.ap[:, sl1 * D:(sl1 + 1) * D],
                         [xq1.sub(sl1)], D, modscr, iscr)
            p.op("v", lambda v, g=g, ro=ro: v.tensor_scalar_mul(
                msc1.ap[:, g:g + 1], rowm.ap[:, ro:ro + 1], 1.0 / 127.0),
                rd=[rowm.sub(sl1)], wr=[msc1.sub(0)])
            p.op("v", lambda v, g=g: v.tensor_mul(
                msc1.ap[:, g:g + 1], msc1.ap[:, g:g + 1], sw1.ap[:, 0:1]),
                rd=[msc1.sub(0), sw1.sub(0)], wr=[msc1.sub(0)])
            iv_to_row(msc1.ap[:, g:g + 1], [msc1.sub(0)], sl1, 0, g)
            if g > 0:
                emit_pets2(g - 1)
            for k in range(KD):
                pet(xq1T.ap[:, k * 128:(k + 1) * 128],
                    xq1.ap[:, sl1 * D + k * 128: sl1 * D + (k + 1) * 128],
                    rd=[xq1.sub(sl1)], wr=[xq1T.sub(0)])
            for nt, (n0, nw) in enumerate(h2tiles):
                bank = nt % 2
                p1i = 32 * (n0 // BCW)
                bc1 = BCW
                for k in range(KD):
                    p.op("t", lambda t, bank=bank, k=k, n0=n0, nw=nw: t.matmul(
                        pb[bank][:, 0:nw], xq1T.ap[:, k * 128:(k + 1) * 128],
                        wq1T.ap[:, k * H2 + n0: k * H2 + n0 + nw],
                        start=(k == 0), stop=(k == KD - 1)),
                        rd=[xq1T.sub(0), wq1T.sub(k * NH5 + nt)], wr=[pb_buf[bank]])
                    if k == 0:
                        p.op("t", lambda t, bank=bank, n0=n0, nw=nw, p1i=p1i, bc1=bc1: t.matmul(
                            pb[bank][:, 0:nw], browT.ap[p1i:p1i + 1, bc1:bc1 + 128],
                            browT.ap[p1i:p1i + 1, n0 % BCW: n0 % BCW + nw],
                            start=False, stop=False, skip_group_check=True),
                            rd=[browT.sub(0)], wr=[pb_buf[bank]])
                p.op("s", lambda s, bank=bank, n0=n0, nw=nw, g=g: s.activation(
                    hth.ap[:, n0:n0 + nw], pb[bank][:, 0:nw], AF.Gelu, bias=0.0,
                    scale=msc1.ap[:, g:g + 1]),
                    rd=[pb_buf[bank], msc1.sub(0)], wr=[hth.sub(nt)])
                for j in range((nt * 512) // D, ((nt + 1) * 512) // D):
                    jb = [hth.sub(s) for s in
                          range((j * D) // 512, ((j + 1) * D - 1) // 512 + 1)]
                    p.op("s", lambda s, j=j, ro=ro: s.activation(
                        sq_scratch.ap[:, 0:D], hth.ap[:, j * D:(j + 1) * D],
                        AF.Square, accum_out=rowm.ap[:, ro + 2 + j: ro + 3 + j]),
                        rd=jb, wr=[sq_scratch.sub(0), rowm.sub(sl1)])
            for j in range(1, nparts):
                p.op("v", lambda v, j=j, ro=ro: v.tensor_add(
                    rowm.ap[:, ro + 2:ro + 3], rowm.ap[:, ro + 2:ro + 3],
                    rowm.ap[:, ro + 2 + j:ro + 3 + j]),
                    rd=[rowm.sub(sl1)], wr=[rowm.sub(sl1)])
            p.op("v", lambda v, ro=ro: v.tensor_scalar_mul(
                rowm.ap[:, ro + 2:ro + 3], rowm.ap[:, ro + 2:ro + 3], 1.0 / H2),
                rd=[rowm.sub(sl1)], wr=[rowm.sub(sl1)])
            p.op("s", lambda s, ro=ro: s.activation(
                rowm.ap[:, ro + 3:ro + 4], rowm.ap[:, ro + 2:ro + 3], AF.Sqrt, bias=EPS6),
                rd=[rowm.sub(sl1), consts.sub(0)], wr=[rowm.sub(sl1)])
            p.op("v", lambda v, ro=ro: v.reciprocal(
                rowm.ap[:, ro + 3:ro + 4], rowm.ap[:, ro + 3:ro + 4]),
                rd=[rowm.sub(sl1)], wr=[rowm.sub(sl1)])
            p.op("v", lambda v, ro=ro: v.tensor_reduce(
                rowm.ap[:, ro + 4:ro + 5], hth.ap, AX.X, ALU.max,
                apply_absolute_value=True),
                rd=list(hth.all_bufs()), wr=[rowm.sub(sl1)])
            p.op("v", lambda v, ro=ro: v.tensor_mul(
                rowm.ap[:, ro + 4:ro + 5], rowm.ap[:, ro + 4:ro + 5],
                rowm.ap[:, ro + 3:ro + 4]),
                rd=[rowm.sub(sl1)], wr=[rowm.sub(sl1)])
            p.op("v", lambda v, ro=ro: v.tensor_scalar_max(
                rowm.ap[:, ro + 4:ro + 5], rowm.ap[:, ro + 4:ro + 5], 1e-5),
                rd=[rowm.sub(sl1)], wr=[rowm.sub(sl1)])
            p.op("v", lambda v, ro=ro: v.reciprocal(
                rowm.ap[:, ro + 5:ro + 6], rowm.ap[:, ro + 4:ro + 5]),
                rd=[rowm.sub(sl1)], wr=[rowm.sub(sl1)])
            p.op("v", lambda v, ro=ro: v.tensor_scalar_mul(
                rowm.ap[:, ro + 5:ro + 6], rowm.ap[:, ro + 5:ro + 6], 127.0),
                rd=[rowm.sub(sl1)], wr=[rowm.sub(sl1)])
            p.op("v", lambda v, ro=ro: v.tensor_mul(
                rowm.ap[:, ro + 5:ro + 6], rowm.ap[:, ro + 5:ro + 6],
                rowm.ap[:, ro + 3:ro + 4]),
                rd=[rowm.sub(sl1)], wr=[rowm.sub(sl1)])
            p.op("v", lambda v, g=g, ro=ro: v.tensor_scalar_mul(
                msc2.ap[:, g:g + 1], rowm.ap[:, ro + 4:ro + 5], 1.0 / 127.0),
                rd=[rowm.sub(sl1)], wr=[msc2.sub(0)])
            p.op("v", lambda v, g=g: v.tensor_mul(
                msc2.ap[:, g:g + 1], msc2.ap[:, g:g + 1], sw2.ap[:, 0:1]),
                rd=[msc2.sub(0), sw2.sub(0)], wr=[msc2.sub(0)])
            iv_to_row(msc2.ap[:, g:g + 1], [msc2.sub(0)], sl1, 1, g)
            hsubs = lambda j0, jw: [hth.sub(s) for s in
                                    range(j0 // 512, (j0 + jw - 1) // 512 + 1)]
            emit_round_q(hth.ap, hsubs, rowm.ap[:, ro + 5:ro + 6],
                         [rowm.sub(sl1)], hth.ap, hsubs, H2, modscr, iscr)
            if g > 0:
                emit_mm2(g - 1)
        emit_pets2(NB * SC - 1)
        emit_mm2(NB * SC - 1)
        for r in (wq1T, wq2T, browT, ivrow, ivb, ivf, xq1, xq1T, hth, xq2T,
                  xnf, res, rowm, modscr, iscr, msc1, msc2, rr):
            mem.free(r)

    nb_emit = cfg.get("nblocks", 4)
    emit_block0()
    if nb_emit >= 2:
        emit_block1()
    if nb_emit >= 3:
        emit_block2()
    if nb_emit >= 4:
        emit_block3()
    p.finish()
    p.build()
    stack.pop_all()
    return


def make_inputs_per_core(inputs, cfg, core):
    NB = cfg["NB"]
    d = {}
    x = inputs["L0_attn_x"]
    d["L0_attn_x"] = np.ascontiguousarray(x[core * NB:(core + 1) * NB])
    for k, v in inputs.items():
        if k != "L0_attn_x":
            d[k] = np.ascontiguousarray(v)
    m = np.zeros((128, 128), dtype=ml_dtypes.bfloat16)
    iu = np.triu_indices(128)
    mm = np.zeros((128, 128), dtype=np.float32)
    mm[iu] = 1.0
    d["cmask"] = mm.astype(ml_dtypes.bfloat16)
    d["cident"] = np.eye(128, dtype=np.float32).astype(ml_dtypes.bfloat16)
    d["cones"] = np.ones((1, 128), dtype=np.float32)
    return d


def kernel(**inputs):
    cfg = REAL
    nc = bass.Bass("TRN2", target_bir_lowering=False, debug=False)
    build_program(nc, cfg)
    in_maps = [make_inputs_per_core(inputs, cfg, c) for c in range(8)]
    res = run_bass_kernel_spmd(nc, in_maps, core_ids=list(range(8)))
    outs = [res.results[c]["out"] for c in range(8)]
    return np.concatenate(outs, axis=0).astype(np.float32)
